# version 12
# baseline (speedup 1.0000x reference)
import numpy as np
import ml_dtypes
import concourse.bass as bass
import concourse.mybir as mybir
from concourse.bass_utils import run_bass_kernel_spmd

F32 = mybir.dt.float32
BF16 = mybir.dt.bfloat16
AF = mybir.ActivationFunctionType
ALU = mybir.AluOpType
AX = mybir.AxisListType
BF = ml_dtypes.bfloat16

D = 1024
S = 16384
NTOK = 2 * S
QS = 4096
E = 8
NG = 4
FF = 2730
FCH = [(i * 128, 128) for i in range(21)] + [(2688, 42)]
NFC = len(FCH)
TG = 2048
NCH = NTOK // 128
SBK = 1024
NSB = S // SBK
NPASS = 2
PQ = QS // NPASS
NQG = PQ // 256
SHIFT = 24.0
RSHIFT = 12.0
NITER = 27
KSEL = 4096.0
EPS = 1e-5

UPTO = "C2"
DEBUG = False

_ORD = ["A", "R", "G", "B", "C1", "C2"]


def _on(ph):
    return _ORD.index(ph) <= _ORD.index(UPTO)


class Bx:
    pass


def build():
    nc = bass.Bass("TRN2", target_bir_lowering=False)
    b = Bx()
    b.nc = nc

    b.x_own = nc.dram_tensor("x_own", [S, D], F32, kind="ExternalInput").ap()
    b.x_oth = nc.dram_tensor("x_oth", [S, D], F32, kind="ExternalInput").ap()
    b.wq = nc.dram_tensor("wq", [D, D], F32, kind="ExternalInput").ap()
    b.wk = nc.dram_tensor("wk", [D, D], F32, kind="ExternalInput").ap()
    b.wv = nc.dram_tensor("wv", [D, D], F32, kind="ExternalInput").ap()
    b.wo = nc.dram_tensor("wo", [D, D], F32, kind="ExternalInput").ap()
    b.wr = nc.dram_tensor("wr", [D, E], F32, kind="ExternalInput").ap()
    b.br = nc.dram_tensor("br", [E, 1], F32, kind="ExternalInput").ap()
    b.wup = nc.dram_tensor("wup", [NG, D, FF], F32, kind="ExternalInput").ap()
    b.bup = nc.dram_tensor("bup", [NG, FF], F32, kind="ExternalInput").ap()
    b.wdn = nc.dram_tensor("wdn", [E, FF, D], F32, kind="ExternalInput").ap()
    b.bdn = nc.dram_tensor("bdn", [E, D], F32, kind="ExternalInput").ap()
    b.y = nc.dram_tensor("y", [QS, D], F32, kind="ExternalOutput").ap()

    dt = nc.dram_tensor
    b.wnat = dt("wnat", [4, D, D], BF16, kind="Internal").ap()
    b.xnhi = dt("xnhi", [NTOK, D], BF16, kind="Internal").ap()
    b.xnlo = dt("xnlo", [NTOK, D], BF16, kind="Internal").ap()
    b.kfm = dt("kfm", [8, 128, S], BF16, kind="Internal").ap()
    b.vtm = dt("vtm", [S, D], BF16, kind="Internal").ap()
    b.qfm = dt("qfm", [8, 128, QS], BF16, kind="Internal").ap()
    b.outn = dt("outn", [QS, D], BF16, kind="Internal").ap()
    b.ypart = dt("ypart", [QS, D], F32, kind="Internal").ap()
    b.logit = dt("logit", [E, NTOK], F32, kind="Internal").ap()
    b.xnic = dt("xnic", [4, 8, 8, 128, 128], BF16, kind="Internal").ap()
    if DEBUG:
        b.dbg_kfm = dt("dbg_kfm", [8, 128, S], BF16, kind="ExternalOutput").ap()
        b.dbg_vtm = dt("dbg_vtm", [S, D], BF16, kind="ExternalOutput").ap()
        b.dbg_qfm = dt("dbg_qfm", [8, 128, QS], BF16, kind="ExternalOutput").ap()
        b.dbg_logit = dt("dbg_logit", [E, NTOK], F32, kind="ExternalOutput").ap()
        b.dbg_xnhi = dt("dbg_xnhi", [NTOK, D], BF16, kind="ExternalOutput").ap()
        b.dbg_xnic = dt("dbg_xnic", [4, 8, 8, 128, 128], BF16, kind="ExternalOutput").ap()
        b.dbg_thr = dt("dbg_thr", [128, E], F32, kind="ExternalOutput").ap()
        b.dbg_sm = dt("dbg_sm", [128, NCH * E], F32, kind="ExternalOutput").ap()
        b.dbg_outn = dt("dbg_outn", [QS, D], BF16, kind="ExternalOutput").ap()
        b.dbg_ypart = dt("dbg_ypart", [QS, D], F32, kind="ExternalOutput").ap()
        b.dbg_mtm = dt("dbg_mtm", [128, 32 * E], F32, kind="ExternalOutput").ap()

    POS = [16384 + 1408]

    def alloc(name, shape, dtyp, at=None):
        if at is not None:
            POS[0] = at
        h = nc.alloc_sbuf_tensor_at(name, shape, dtyp, offset=POS[0])
        n = int(np.prod(shape[1:])) * (4 if dtyp == F32 else 2)
        POS[0] = (POS[0] + n + 31) & ~31
        return h

    b.identf = alloc("identf", [128, 128], F32)
    b.colio = alloc("colio", [128, 128], F32)
    b.rowio = alloc("rowio", [128, 1], F32)
    b.wrhi = alloc("wrhi", [128, 8 * E], BF16)
    b.wrlo = alloc("wrlo", [128, 8 * E], BF16)
    b.wrf = alloc("wrf", [128, 8 * E], F32)
    b.brsb = alloc("brsb", [E, 1], F32)
    b.ltm = alloc("ltm", [128, NCH * E], F32)
    b.theta = alloc("theta", [1, E], F32)
    b.threp = alloc("threp", [128, E], F32)
    b.mtm = alloc("mtm", [128, 32 * E], F32)
    OFF_MFM = POS[0]
    b.mfm = alloc("mfm", [E, QS], BF16)
    b.cnt = alloc("cnt", [1, E], F32)
    b.part8 = alloc("part8", [128, E], F32)
    b.g8 = alloc("g8", [128, E], F32)
    b.ones1 = alloc("ones1", [1, 128], F32)
    b.onesc = alloc("onesc", [128, 1], F32)
    b.onesb = alloc("onesb", [128, 1], BF16)
    b.bdb = alloc("bdb", [E, D], BF16)
    b.ustr = alloc("ustr", [128, E], F32)
    b.cm12 = alloc("cm12", [128, 1], F32)
    b.cm24 = alloc("cm24", [128, 1], F32)
    b.c0125 = alloc("c0125", [128, 1], F32)
    b.z8 = alloc("z8", [128, 2], F32)
    OFF_LSTR = POS[0]
    b.lstr = alloc("lstr", [E, 128], F32)
    PERS = POS[0]

    b.wqT = alloc("wqT", [128, 8 * D], BF16, at=PERS)
    b.wkT = alloc("wkT", [128, 8 * D], BF16)
    b.wvT = alloc("wvT", [128, 8 * D], BF16)
    b.hifm = [alloc("hifm0", [128, 8 * TG], BF16),
              alloc("hifm1", [128, 8 * TG], BF16)]
    OFF_LOFM = POS[0]
    b.lofm = alloc("lofm", [128, 8 * TG], BF16)
    b.xr = [alloc("xr%d" % i, [128, D], F32) for i in range(5)]
    b.hitm = [alloc("hitm0", [128, D], BF16), alloc("hitm1", [128, D], BF16)]
    b.lotm = [alloc("lotm0", [128, D], BF16), alloc("lotm1", [128, D], BF16)]
    b.bn12 = [alloc("bn12_0", [128, 12], F32), alloc("bn12_1", [128, 12], F32)]
    b.statm = [alloc("statm0", [128, 96], F32), alloc("statm1", [128, 96], F32)]
    b.kst = [alloc("kst0", [128, 512], BF16), alloc("kst1", [128, 512], BF16)]
    b.vst = [alloc("vst0", [128, D], BF16), alloc("vst1", [128, D], BF16)]
    b.qst = [alloc("qst0", [128, 512], BF16), alloc("qst1", [128, 512], BF16)]
    b.xist = [alloc("xist0", [128, 128], BF16), alloc("xist1", [128, 128], BF16)]
    endA = POS[0]
    b.wld = alloc("wld", [128, 8 * D], BF16, at=OFF_LOFM)

    b.smtm = alloc("smtm", [128, NCH * E], F32, at=PERS)
    b.ztm = alloc("ztm", [128, NCH], F32)
    b.ind = alloc("ind", [128, NCH], F32)
    b.qsb = alloc("qsb", [128, 8 * PQ], BF16)
    b.osb = alloc("osb", [128, 16 * D], F32)
    b.ksb = [alloc("ksb0", [128, 8 * SBK], BF16),
             alloc("ksb1", [128, 8 * SBK], BF16)]
    b.vsb = [alloc("vsb0", [128, 8 * SBK], BF16),
             alloc("vsb1", [128, 8 * SBK], BF16)]
    b.pT = [alloc("pT0", [128, 256], BF16), alloc("pT1", [128, 256], BF16)]
    b.lsb = alloc("lsb", [128, 16], F32)
    b.lrec = alloc("lrec", [128, 16], F32)
    b.onst = [alloc("onst0", [128, D], BF16), alloc("onst1", [128, D], BF16)]
    endB = POS[0]

    b.woT = alloc("woT", [128, 8 * D], BF16, at=PERS)
    b.onld = alloc("onld", [128, 8 * D], BF16)
    b.xres = [alloc("xres0", [128, D], F32), alloc("xres1", [128, D], F32)]
    b.ypst = [alloc("ypst0", [128, D], F32), alloc("ypst1", [128, D], F32)]
    endC1 = POS[0]

    b.wupsb = alloc("wupsb", [128, 8 * FF], BF16, at=PERS)
    b.wdsb = [alloc("wdsb0", [128, NFC * D], BF16),
              alloc("wdsb1", [128, NFC * D], BF16)]
    b.mid = alloc("mid", [128, NFC * 512], BF16)
    b.xmv = [alloc("xmv0", [128, 8 * 512], BF16),
             alloc("xmv1", [128, 8 * 512], BF16)]
    b.ysb = alloc("ysb", [128, 4 * D], F32)
    endC2 = POS[0]
    b.rst = [alloc("rst0", [E, 512], F32, at=OFF_MFM),
             alloc("rst1", [E, 512], F32)]
    b.rls = [alloc("rls0", [E, 512], F32),
             alloc("rls1", [E, 512], F32)]
    b.bupsb = alloc("bupsb", [128, NG * NFC], F32, at=OFF_LSTR)

    hwm = max(endA, endB, endC1, endC2)
    assert hwm <= 229344, f"SBUF over budget: {hwm}"

    b.PA = nc.alloc_psum_tensor("PA", [128, 512], F32)
    b.PA2 = nc.alloc_psum_tensor("PA2", [128, 512], F32)
    b.PB = nc.alloc_psum_tensor("PB", [128, 512], F32)
    b.PB2 = nc.alloc_psum_tensor("PB2", [128, 512], F32)
    b.PC = nc.alloc_psum_tensor("PC", [128, 512], F32)
    b.PD = nc.alloc_psum_tensor("PD", [128, 512], F32)
    b.PE = nc.alloc_psum_tensor("PE", [128, 512], F32)
    b.PF = nc.alloc_psum_tensor("PF", [128, 512], F32)

    from contextlib import ExitStack
    with ExitStack() as ctx:
        for name in ["v_bn", "s_sq", "v_bs", "v_xn", "s_hi", "g_lo", "s_ts",
                     "t_k", "s_ke", "t_v", "s_ve",
                     "t_q", "s_qe", "t_r", "s_re", "q_lt", "t_lt", "s_lt",
                     "q_rd", "t_a", "v_xe", "q_w", "q_wl", "g_c", "q_g",
                     "v_c", "s_c", "t_c2", "q_qsb", "t_sb", "t_s", "s_e",
                     "t_pv", "v_m", "v_r", "s_n", "t_c", "v_yp",
                     "q_yp", "q_y", "t_u", "s_u", "t_d", "v_g", "t_b",
                     "v_fin", "s_fin", "v_bb", "q_dbg", "q_wT", "q_lo",
                     "q_on2", "q_c2c", "q_wt"]:
            setattr(b, name, ctx.enter_context(nc.semaphore(name)))
        for name in ["q_st", "q_tl", "q_ke", "q_ve", "q_qe",
                     "q_xs", "q_kv", "q_on", "q_xm"]:
            setattr(b, name, [ctx.enter_context(nc.semaphore(name + "0")),
                              ctx.enter_context(nc.semaphore(name + "1"))])
        b.q_xr = [ctx.enter_context(nc.semaphore("q_xr%d" % i))
                  for i in range(5)]
        b.q_ls = [ctx.enter_context(nc.semaphore("q_ls0")),
                  ctx.enter_context(nc.semaphore("q_ls1"))]
        b.q_wq = [ctx.enter_context(nc.semaphore("q_wq%d" % i))
                  for i in range(4)]
        b.q_wd = ctx.enter_context(nc.semaphore("q_wd"))
        blk = ctx.enter_context(nc.Block())

        @blk.sync
        def _(e):
            _sync(b, e)

        @blk.scalar
        def _(e):
            _scalar(b, e)

        @blk.vector
        def _(e):
            _vector(b, e)

        @blk.tensor
        def _(e):
            _tensor(b, e)

        @blk.gpsimd
        def _(e):
            _gpsimd(b, e)
    return nc


def _xsrc(b, n):
    tg, i = divmod(n, 16)
    base = (tg % 8) * TG + i * 128
    src = b.x_own if tg < 8 else b.x_oth
    return src[base:base + 128, :]


def _own(tg):
    return tg < 8


def _qkv_b(p, sb):
    return 16 * (p * (8 + 16 * NSB) + 8 + (sb + 1) * 16)


def _qkv_c1(t):
    return 16 * (2 * (8 + 16 * NSB) + 8 * t + 8)


def _qg_xres(cu):
    return 16 * (2 + 32 + cu + 1)


def _qg_lstr(st):
    return 16 * (2 + st + 1)


def _tl_hi(b, e, htg, c, first):
    if first:
        if htg >= 2:
            e.wait_ge(b.t_r, 4 * (htg - 1))
        e.wait_ge(b.q_st[0], 32 * 8 * (htg + 1))
        e.wait_ge(b.q_st[1], 32 * 8 * (htg + 1))
        e.nop().then_inc(b.s_ts, 1)
        if htg >= 2 and _own(htg - 2):
            e.wait_ge(b.v_xe, 32 * (htg - 1))
    e.dma_start(b.hifm[htg % 2][:, c * TG:(c + 1) * TG],
                b.xnhi[htg * TG:(htg + 1) * TG, c * 128:(c + 1) * 128],
                transpose=True).then_inc(b.q_tl[htg % 2], 16)


def _tl_lo(b, e, ltg, c, first):
    if first:
        if ltg >= 1:
            e.wait_ge(b.t_r, 4 * ltg)
        else:
            e.wait_ge(b.q_w, 16 * 4)
        e.wait_ge(b.q_st[0], 32 * 8 * (ltg + 1))
        e.wait_ge(b.q_st[1], 32 * 8 * (ltg + 1))
    e.dma_start(b.lofm[:, c * TG:(c + 1) * TG],
                b.xnlo[ltg * TG:(ltg + 1) * TG, c * 128:(c + 1) * 128],
                transpose=True).then_inc(b.q_lo, 16)


def _lt_evac(b, e, g):
    e.wait_ge(b.t_lt, g + 1)
    pr = b.PB if g % 2 == 0 else b.PB2
    e.tensor_copy(b.ltm[:, g * 4 * E:(g + 1) * 4 * E],
                  pr[:, 0:32]).then_inc(b.s_lt, 1)


def _stage_tr(b, e, g):
    e.wait_ge(b.q_ls[g % 2], 16 * (g // 2 + 1))
    pr = b.PB if g % 2 == 0 else b.PB2
    if g >= 2:
        e.wait_ge(b.s_lt, g - 1)
    for j in range(4):
        ins = e.transpose(pr[:, j * E:(j + 1) * E],
                          b.rls[g % 2][0:E, j * 128:(j + 1) * 128],
                          b.identf[0:E, 0:E])
    ins.then_inc(b.t_lt, 1)


def _c2_up_bank(b, gg, uu):
    if gg == 0:
        return (b.PC, b.PD)[uu % 2]
    return (b.PC, b.PD, b.PE, b.PF)[uu % 4]


def _qg_c2w(gg):
    return 16 * (2 + 32 + 32 + 1 + 8 + gg * 37 + 5)


def _qg_c2x(gg, tt):
    return 16 * (2 + 32 + 32 + 1 + 8 + gg * 37 + 5 + (tt + 1) * 32)


def _sync(b, e):
    for j, wT in enumerate([b.wqT, b.wkT, b.wvT]):
        e.wait_ge(b.q_w, 16 * (j + 1))
        for dc in range(8):
            e.dma_start(wT[:, dc * D:(dc + 1) * D],
                        b.wnat[j, :, dc * 128:(dc + 1) * 128],
                        transpose=True).then_inc(b.q_wT, 16)

    def stage_load(g):
        e.wait_ge(b.q_rd, 16 * min(64, g + 2))
        if g >= 2:
            e.wait_ge(b.t_lt, g - 1)
        e.dma_start(b.rls[g % 2][:, :],
                    b.logit[:, g * 512:(g + 1) * 512]
                    ).then_inc(b.q_ls[g % 2], 16)

    for tau in range(18):
        if 0 <= tau - 2 <= 15:
            for c in range(8):
                _tl_hi(b, e, tau - 2, c, c == 0)
            for c in range(8):
                _tl_lo(b, e, tau - 2, c, c == 0)
        if _on("R") and 0 <= tau - 8 <= 9:
            for gi in range(4):
                stage_load(4 * (tau - 8) + gi)
    if _on("R"):
        for g in range(40, 64):
            stage_load(g)

    if not _on("B"):
        e.wait_ge(b.q_wT, 16 * 24)
        e.wait_ge(b.q_lo, 16 * 128)
        for par in range(2):
            e.wait_ge(b.q_tl[par], 16 * 64)
            e.wait_ge(b.q_ke[par], 16 * 128)
            e.wait_ge(b.q_ve[par], 16 * 128)
            e.wait_ge(b.q_qe[par], 16 * 32)
            e.wait_ge(b.q_xs[par], 16 * 128)
        e.wait_ge(b.q_rd, 16 * 64)
        if _on("R"):
            e.wait_ge(b.s_c, 2 * NITER + 2)
        _dbg_taps(b, e)
        return

    for p in range(NPASS):
        if p == 0:
            e.wait_ge(b.t_r, 64)
            e.wait_ge(b.q_xs[0], 16 * 128)
            e.wait_ge(b.q_xs[1], 16 * 128)
            for par in range(2):
                e.wait_ge(b.q_ke[par], 16 * 128)
                e.wait_ge(b.q_ve[par], 16 * 128)
                e.wait_ge(b.q_qe[par], 16 * 32)
            e.wait_ge(b.s_c, 2 * NITER + 2)
        else:
            e.wait_ge(b.t_pv, 8 * NSB)
        for dc in range(8):
            e.dma_start(b.qsb[:, dc * PQ:(dc + 1) * PQ],
                        b.qfm[dc, :, p * PQ:(p + 1) * PQ]).then_inc(b.q_qsb, 16)
        for sb in range(NSB):
            g = p * NSB + sb
            par = g % 2
            if g >= 2:
                e.wait_ge(b.t_pv, 8 * (g - 1))
            for hc in range(8):
                e.dma_start(b.ksb[par][:, hc * SBK:(hc + 1) * SBK],
                            b.kfm[hc, :, sb * SBK:(sb + 1) * SBK]
                            ).then_inc(b.q_kv[par], 16)
            for tc in range(8):
                row = sb * SBK + tc * 128
                e.dma_start(b.vsb[par][:, tc * SBK:(tc + 1) * SBK],
                            b.vtm[row:row + 128, :]).then_inc(b.q_kv[par], 16)

    if not _on("C1"):
        e.wait_ge(b.q_on[0], 16 * 16)
        e.wait_ge(b.q_on[1], 16 * 16)
        _dbg_taps(b, e)
        return

    e.wait_ge(b.q_on[0], 16 * 16)
    e.wait_ge(b.q_on[1], 16 * 16)
    e.wait_ge(b.q_w, 16 * 4)
    for dc in range(8):
        e.dma_start(b.woT[:, dc * D:(dc + 1) * D],
                    b.wnat[3, :, dc * 128:(dc + 1) * 128],
                    transpose=True).then_inc(b.q_wT, 16)
    for t in range(4):
        if t >= 1:
            e.wait_ge(b.t_c, 8 * t)
        for jc in range(8):
            row = t * 1024 + jc * 128
            e.dma_start(b.onld[:, jc * D:(jc + 1) * D],
                        b.outn[row:row + 128, :]).then_inc(b.q_on2, 16)

    if not _on("C2"):
        e.wait_ge(b.q_yp, 16 * 32)
        if _on("G"):
            e.wait_ge(b.s_fin, 64)
        _dbg_taps(b, e)
        return

    for gg in range(NG):
        for tt in range(8):
            t = tt // 2
            e.wait_ge(b.q_yp, 16 * (32 + gg * 32 + tt * 4))
            for tc in range(4):
                r = 4 * (tt % 2) + tc
                row0 = t * 1024 + r
                e.dma_start(b.ysb[:, tc * D:(tc + 1) * D],
                            b.ypart[row0:row0 + 1017:8, :]).then_inc(b.q_y, 16)
    e.wait_ge(b.q_yp, 16 * (32 + NG * 32))
    _dbg_taps(b, e)


def _dbg_taps(b, e):
    if not DEBUG:
        return
    n = 0
    pairs = []
    if _on("A"):
        pairs += [(b.dbg_kfm, b.kfm), (b.dbg_vtm, b.vtm), (b.dbg_qfm, b.qfm),
                  (b.dbg_logit, b.logit), (b.dbg_xnhi, b.xnhi),
                  (b.dbg_xnic, b.xnic)]
    if _on("B"):
        pairs += [(b.dbg_outn, b.outn)]
    if _on("C1"):
        pairs += [(b.dbg_ypart, b.ypart)]
    for dst, src in pairs:
        e.dma_start(dst, src).then_inc(b.q_dbg, 16)
        n += 1
    if _on("R"):
        e.dma_start(b.dbg_thr, b.threp[:, :]).then_inc(b.q_dbg, 16)
        n += 1
        if not _on("C1"):
            e.dma_start(b.dbg_sm, b.smtm[:, :]).then_inc(b.q_dbg, 16)
            n += 1
    if _on("G"):
        e.dma_start(b.dbg_mtm, b.mtm[:, :]).then_inc(b.q_dbg, 16)
        n += 1
    e.wait_ge(b.q_dbg, 16 * n)


def _scalar(b, e):
    ku = vu = qu = ru = 0

    def k_unit(utg):
        nonlocal ku
        kp = ku % 2
        e.wait_ge(b.t_k, ku + 1)
        if ku >= 2:
            e.wait_ge(b.q_ke[kp], 16 * (ku // 2))
        ps = b.PC if kp == 0 else b.PD
        e.activation(b.kst[kp][:, :], ps[:, :], AF.Copy)
        e.drain()
        e.nop().then_inc(b.s_ke, 1)
        g512, hc = divmod(ku % 32, 8)
        e.dma_start(b.kfm[hc, :, utg * TG + g512 * 512:
                          utg * TG + (g512 + 1) * 512],
                    b.kst[kp][:, :]).then_inc(b.q_ke[kp], 16)
        ku += 1

    def v_unit(utg):
        nonlocal vu
        vp = vu % 2
        e.wait_ge(b.t_v, vu + 1)
        if vu >= 2:
            e.wait_ge(b.q_ve[vp], 16 * (vu // 2))
        ps = b.PA if vp == 0 else b.PA2
        e.activation(b.vst[vp][:, 0:512], ps[:, :], AF.Copy)
        e.drain()
        e.nop().then_inc(b.s_ve, 1)
        idx = vu % 32
        row = utg * TG + (idx // 2) * 128
        h = idx % 2
        e.dma_start(b.vtm[row:row + 128, h * 512:h * 512 + 512],
                    b.vst[vp][:, 0:512]).then_inc(b.q_ve[vp], 16)
        vu += 1

    def q_unit(utg):
        nonlocal qu
        qp = qu % 2
        e.wait_ge(b.t_q, qu + 1)
        if qu >= 2:
            e.wait_ge(b.q_qe[qp], 16 * (qu // 2))
        ps = b.PB if qp == 0 else b.PB2
        e.activation(b.qst[qp][:, :], ps[:, :], AF.Copy)
        e.drain()
        e.nop().then_inc(b.s_qe, 1)
        g512, hc = divmod(qu % 32, 8)
        e.dma_start(b.qfm[hc, :, utg * TG + g512 * 512:
                          utg * TG + (g512 + 1) * 512],
                    b.qst[qp][:, :]).then_inc(b.q_qe[qp], 16)
        qu += 1

    def r_unit(utg):
        nonlocal ru
        rp = ru % 2
        e.wait_ge(b.t_r, ru + 1)
        if ru == 0:
            e.wait_ge(b.q_g, 32)
        if ru >= 2:
            e.wait_ge(b.q_rd, 16 * ru)
        ps = b.PE if rp == 0 else b.PF
        e.activation(b.rst[rp][:, :], ps[0:E, :], AF.Identity,
                     bias=b.brsb[:, 0:1])
        e.drain()
        e.nop().then_inc(b.s_re, 1)
        g512 = ru % 4
        base = utg * TG + g512 * 512
        e.dma_start(b.logit[:, base:base + 512],
                    b.rst[rp][:, :]).then_inc(b.q_rd, 16)
        ru += 1

    def l_unit(g):
        e.wait_ge(b.t_lt, g + 1)
        pr = b.PB if g % 2 == 0 else b.PB2
        e.activation(b.ltm[:, g * 4 * E:(g + 1) * 4 * E], pr[:, 0:32],
                     AF.Copy).then_inc(b.s_lt, 1)

    queues = []
    for tg in range(16):
        q = []
        if _own(tg):
            for st in range(32):
                q.append(("K", tg))
                q.append(("V", tg))
                if tg < 2:
                    q.append(("Q", tg))
        q += [("R", tg)] * 4
        if _on("R") and 5 <= tg <= 15:
            q += [("L", 4 * (tg - 5) + gi) for gi in range(4)]
        queues.append(q)
    emit = {"K": k_unit, "V": v_unit, "Q": q_unit, "R": r_unit,
            "L": l_unit}

    for T in range(19):
        for i in range(16):
            if 1 <= T <= 16:
                m = 16 * (T - 1) + i
                par = m % 2
                if m % 2 == 0:
                    p = m // 2
                    c = 2 * (p % 8)
                    smp = b.statm[(m // 16) % 2]
                    e.wait_ge(b.v_bs, p + 1)
                    e.activation(smp[:, 48 + c:50 + c],
                                 smp[:, 32 + c:34 + c], AF.Sqrt)
                    e.drain()
                    e.nop().then_inc(b.s_sq, 1)
                e.wait_ge(b.v_xn, m + 1)
                if m >= 2:
                    e.wait_ge(b.q_st[par], 32 * (m // 2))
                e.activation(b.hitm[par][:, :], b.xr[m % 5][:, :],
                             AF.Copy).then_inc(b.s_hi, 1)
            if T >= 3:
                qq = queues[T - 3]
                for _ in range(min(7, len(qq))):
                    if qq[0][0] == "R" and i < 8:
                        break
                    kind, utg = qq.pop(0)
                    emit[kind](utg)
        if T >= 3:
            for kind, utg in queues[T - 3]:
                emit[kind](utg)
            queues[T - 3] = []

    if not _on("R"):
        return
    for g in range(44, 64):
        l_unit(g)
    e.drain()
    e.wait_ge(b.v_c, 1)
    e.activation(b.smtm[:, :], b.ltm[:, :], AF.Exp,
                 bias=b.cm12[:, 0:1]).then_inc(b.s_c, 1)
    for it in range(NITER):
        e.wait_ge(b.t_c2, 2 * it + 1)
        e.activation(b.cnt[0:1, :], b.PE[0:1, 0:E], AF.Copy).then_inc(b.s_c, 1)
        e.wait_ge(b.t_c2, 2 * it + 2)
        e.activation(b.threp[:, :], b.PF[:, 0:E], AF.Copy).then_inc(b.s_c, 1)
    e.wait_ge(b.t_c2, 2 * NITER + 1)
    e.activation(b.threp[:, :], b.PF[:, 0:E], AF.Copy).then_inc(b.s_c, 1)

    if not _on("G"):
        return
    for st in range(32):
        e.wait_ge(b.t_b, 2 * st + 1)
        if st >= 1:
            e.wait_ge(b.v_fin, st)
        e.activation(b.ustr[:, :], b.PF[:, 0:E], AF.Exp,
                     bias=b.cm12[:, 0:1]).then_inc(b.s_fin, 1)
        e.wait_ge(b.t_b, 2 * st + 2)
        e.activation(b.mfm[:, st * 128:(st + 1) * 128], b.PE[0:E, 0:128],
                     AF.Copy).then_inc(b.s_fin, 1)

    if not _on("B"):
        return
    for p in range(NPASS):
        for sb in range(NSB):
            for qg in range(NQG):
                for kt in range(8):
                    su = ((p * NSB + sb) * NQG + qg) * 8 + kt
                    sp = su % 2
                    e.wait_ge(b.t_s, su + 1)
                    ps = b.PC if sp == 0 else b.PD
                    e.activation(b.pT[sp][:, :], ps[:, 0:256], AF.Exp,
                                 bias=b.cm24[:, 0:1],
                                 scale=b.c0125[:, 0:1]).then_inc(b.s_e, 1)
        e.wait_ge(b.v_r, p + 1)
        for tile in range(16):
            gt = p * 16 + tile
            if gt >= 2:
                e.wait_ge(b.q_on[gt % 2], 16 * (gt // 2))
            e.activation(b.onst[gt % 2][:, :],
                         b.osb[:, tile * D:(tile + 1) * D], AF.Copy,
                         scale=b.lrec[:, tile:tile + 1])
            e.drain()
            e.nop().then_inc(b.s_n, 1)
            row = p * PQ + tile * 128
            e.dma_start(b.outn[row:row + 128, :],
                        b.onst[gt % 2][:, :]).then_inc(b.q_on[gt % 2], 16)

    if not _on("C1"):
        return
    for cu in range(32):
        e.wait_ge(b.v_yp, cu + 1)
        t, ddc = divmod(cu, 8)
        row = t * 1024 + ddc * 128
        e.dma_start(b.ypart[row:row + 128, :],
                    b.ypst[cu % 2][:, :]).then_inc(b.q_yp, 16)

    if not _on("C2"):
        return
    uu = 0
    e.wait_ge(b.q_c2c, 16 * 9)
    for gg in range(NG):
        for tt in range(8):
            for fi, (f0, fw) in enumerate(FCH):
                e.wait_ge(b.t_u, uu + 1)
                if fi == 0:
                    e.wait_ge(b.t_d, (gg * 8 + tt) * 8)
                ps = _c2_up_bank(b, gg, uu)
                e.activation(b.mid[0:fw, fi * 512:(fi + 1) * 512],
                             ps[0:fw, :], AF.Silu,
                             bias=b.bupsb[0:fw, gg * NFC + fi:gg * NFC + fi + 1]
                             ).then_inc(b.s_u, 1)
                uu += 1
            e.wait_ge(b.v_g, (gg * 8 + tt) * 8 + 8)
            t = tt // 2
            for tc in range(4):
                r = 4 * (tt % 2) + tc
                if gg < NG - 1:
                    row0 = t * 1024 + r
                    e.dma_start(b.ypart[row0:row0 + 1017:8, :],
                                b.ysb[:, tc * D:(tc + 1) * D]).then_inc(b.q_yp, 16)
                else:
                    row = tt * 512 + tc * 128
                    e.dma_start(b.y[row:row + 128, :],
                                b.ysb[:, tc * D:(tc + 1) * D]).then_inc(b.q_yp, 16)
    e.wait_ge(b.q_yp, 16 * (32 + NG * 32))


def _vector(b, e):
    e.wait_ge(b.g_c, 2)
    e.tensor_scalar(b.identf[:, :], b.colio[:, :], b.rowio[:, 0:1], None,
                    op0=ALU.is_equal)
    e.memset(b.ones1[:, :], 1.0)
    e.memset(b.onesc[:, :], 1.0)
    e.memset(b.onesb[:, :], 1.0)
    e.memset(b.theta[:, :], 0.5)
    e.memset(b.threp[:, :], 0.5)
    e.memset(b.cm12[:, :], -RSHIFT)
    e.memset(b.cm24[:, :], -SHIFT)
    e.memset(b.c0125[:, :], 0.125)
    e.wait_ge(b.q_g, 32)
    e.tensor_copy(b.wrhi[:, :], b.wrf[:, :])
    e.drain()
    e.tensor_tensor(b.wrlo[:, :], b.wrf[:, :], b.wrhi[:, :], op=ALU.subtract)
    e.engine_nop().then_inc(b.v_c, 1)

    xe = 0
    for k in range(259):
        tg, i = divmod(k, 16)
        m = k - 2
        if 0 <= m <= 255:
            mtg, mj = divmod(m, 16)
            smp = b.statm[mtg % 2]
            if m % 2 == 0:
                p = m // 2
                c = 2 * (p % 8)
                e.wait_ge(b.s_sq, p + 1)
                e.reciprocal(smp[:, 64 + c:66 + c], smp[:, 48 + c:50 + c])
                e.drain()
                e.scalar_tensor_tensor(smp[:, 80 + c:82 + c],
                                       smp[:, 2 * c:2 * c + 3:2], -1.0,
                                       smp[:, 64 + c:66 + c],
                                       op0=ALU.mult, op1=ALU.mult)
                e.drain()
            e.tensor_scalar(b.xr[m % 5][:, :], b.xr[m % 5][:, :],
                            smp[:, 64 + mj:65 + mj], smp[:, 80 + mj:81 + mj],
                            op0=ALU.mult, op1=ALU.add).then_inc(b.v_xn, 1)
            e.drain()
        sj = k - 3
        if 0 <= sj <= 255:
            e.wait_ge(b.s_hi, sj + 1)
            e.tensor_tensor(b.lotm[sj % 2][:, :], b.xr[sj % 5][:, :],
                            b.hitm[sj % 2][:, :],
                            op=ALU.subtract).then_inc(b.g_lo, 1)
        if k <= 255:
            par = k % 2
            sm = b.statm[tg % 2]
            e.wait_ge(b.q_xr[k % 5], 16 * (k // 5 + 1))
            e.bn_stats(b.bn12[par][:, 0:6], b.xr[k % 5][:, 0:512])
            e.bn_stats(b.bn12[par][:, 6:12], b.xr[k % 5][:, 512:1024])
            e.drain()
            e.bn_aggr(sm[:, 2 * i:2 * i + 2],
                      b.bn12[par][:, :]).then_inc(b.v_bn, 1)
            if k % 2 == 1:
                p = k // 2
                c = 2 * (p % 8)
                e.drain()
                e.tensor_scalar(sm[:, 32 + c:34 + c],
                                sm[:, 2 * c + 1:2 * c + 4:2], EPS, None,
                                op0=ALU.add)
                e.drain()
                e.engine_nop().then_inc(b.v_bs, 1)
        exn = (tg - 2) if i >= 12 else (tg - 3) if i < 4 else -1
        if 0 <= exn <= 7 and (i >= 12 or i < 4):
            ptg = exn
            if i == 12:
                e.wait_ge(b.q_tl[ptg % 2], 16 * 8 * (ptg // 2 + 1))
            for _ in range(4):
                u = xe
                t, r = divmod(u % 32, 8)
                xp = u % 2
                if u >= 2:
                    e.wait_ge(b.q_xs[u % 2], 16 * (u // 2))
                src = b.hifm[ptg % 2][:, :].rearrange("p (c t) -> p c t", t=TG)
                src = src[:, :, 16 * r + t::128]
                e.tensor_copy(
                    b.xist[xp][:, :].rearrange("p (c h) -> p c h", h=16),
                    src).then_inc(b.v_xe, 1)
                xe += 1

    if not _on("R"):
        return
    e.wait_ge(b.s_c, 1)
    sm3 = b.smtm[:, :].rearrange("p (c e) -> p c e", e=E)
    e.tensor_reduce(b.ztm[:, :], sm3, axis=AX.X, op=ALU.add)
    e.drain()
    e.reciprocal(b.ztm[:, :], b.ztm[:, :])
    e.drain()
    for ee in range(E):
        sl = b.smtm[:, :].rearrange("p (c e) -> p c e", e=E)[:, :, ee]
        e.tensor_tensor(sl, sl, b.ztm[:, :], op=ALU.mult)
    e.drain()
    e.engine_nop().then_inc(b.v_c, 1)
    delta = 0.25
    for it in range(NITER):
        if it >= 1:
            e.wait_ge(b.s_c, 1 + 2 * it)
        for ee in range(E):
            sl = b.smtm[:, :].rearrange("p (c e) -> p c e", e=E)[:, :, ee]
            if ee >= 1:
                e.drain()
            e.tensor_scalar(b.ind[:, :], sl, b.threp[:, ee:ee + 1], 0.0,
                            op0=ALU.is_ge, op1=ALU.add,
                            accum_out=b.part8[:, ee:ee + 1])
        e.drain()
        e.engine_nop().then_inc(b.v_c, 1)
        e.wait_ge(b.s_c, 2 + 2 * it)
        if it >= 1:
            e.wait_ge(b.t_c2, 2 * it)
        e.tensor_scalar(b.cnt[0:1, :], b.cnt[0:1, :], KSEL + 0.5, None,
                        op0=ALU.is_ge)
        e.drain()
        e.scalar_tensor_tensor(b.theta[0:1, :], b.cnt[0:1, :], 2 * delta,
                               b.theta[0:1, :], op0=ALU.mult, op1=ALU.add)
        e.drain()
        e.tensor_scalar(b.theta[0:1, :], b.theta[0:1, :], -delta, None,
                        op0=ALU.add).then_inc(b.v_c, 1)
        delta *= 0.5
    e.drain()
    e.wait_ge(b.t_c2, 2 * NITER)
    e.tensor_scalar(b.theta[0:1, :], b.theta[0:1, :], -2 * delta, None,
                    op0=ALU.add).then_inc(b.v_c, 1)

    if not _on("G"):
        return
    for st in range(32):
        e.wait_ge(b.s_fin, 2 * st + 1)
        e.tensor_reduce(b.z8[:, 0:1], b.ustr[:, :], axis=AX.X, op=ALU.add)
        e.drain()
        e.reciprocal(b.z8[:, 1:2], b.z8[:, 0:1])
        e.drain()
        sl = b.mtm[:, st * E:(st + 1) * E]
        e.tensor_scalar(sl, b.ustr[:, :], b.z8[:, 1:2], None, op0=ALU.mult)
        e.drain()
        e.tensor_tensor(b.g8[:, :], sl, b.threp[:, :], op=ALU.is_ge)
        e.drain()
        e.tensor_tensor(sl, sl, b.g8[:, :], op=ALU.mult).then_inc(b.v_fin, 1)

    if not _on("B"):
        return
    for p in range(NPASS):
        if p == 1:
            e.wait_ge(b.s_n, 16)
        else:
            for par in range(2):
                e.wait_ge(b.q_ke[par], 16 * 128)
                e.wait_ge(b.q_ve[par], 16 * 128)
                e.wait_ge(b.q_qe[par], 16 * 32)
                e.wait_ge(b.q_xs[par], 16 * 128)
        e.memset(b.osb[:, :], 0.0)
        e.memset(b.lsb[:, :], 0.0)
        e.engine_nop().then_inc(b.v_m, 1)
        for sb in range(NSB):
            for qg in range(NQG):
                pu = (p * NSB + sb) * NQG + qg
                e.wait_ge(b.t_pv, pu + 1)
                e.drain()
                t0 = 2 * qg
                e.tensor_tensor(b.osb[:, t0 * D:t0 * D + 512],
                                b.PA[:, :], b.osb[:, t0 * D:t0 * D + 512],
                                op=ALU.add)
                e.tensor_tensor(b.osb[:, t0 * D + 512:(t0 + 1) * D],
                                b.PA2[:, :],
                                b.osb[:, t0 * D + 512:(t0 + 1) * D],
                                op=ALU.add)
                e.tensor_tensor(b.osb[:, (t0 + 1) * D:(t0 + 1) * D + 512],
                                b.PB[:, :],
                                b.osb[:, (t0 + 1) * D:(t0 + 1) * D + 512],
                                op=ALU.add)
                e.tensor_tensor(b.osb[:, (t0 + 1) * D + 512:(t0 + 2) * D],
                                b.PB2[:, :],
                                b.osb[:, (t0 + 1) * D + 512:(t0 + 2) * D],
                                op=ALU.add)
                e.tensor_tensor(b.lsb[:, t0:t0 + 1], b.PE[:, qg:qg + 1],
                                b.lsb[:, t0:t0 + 1], op=ALU.add)
                e.tensor_tensor(b.lsb[:, t0 + 1:t0 + 2], b.PF[:, qg:qg + 1],
                                b.lsb[:, t0 + 1:t0 + 2],
                                op=ALU.add).then_inc(b.v_m, 1)
        e.drain()
        e.reciprocal(b.lrec[:, :], b.lsb[:, :])
        e.drain()
        e.engine_nop().then_inc(b.v_r, 1)

    if not _on("C1"):
        return
    for cu in range(32):
        e.wait_ge(b.t_c, cu + 1)
        e.wait_ge(b.q_g, _qg_xres(cu))
        cp = cu % 2
        pc = (b.PA, b.PA2) if cp == 0 else (b.PB, b.PB2)
        if cu >= 2:
            e.wait_ge(b.q_yp, 16 * cu)
        e.tensor_tensor(b.ypst[cp][:, 0:512], pc[0][:, :],
                        b.xres[cp][:, 0:512], op=ALU.add)
        e.tensor_tensor(b.ypst[cp][:, 512:1024], pc[1][:, :],
                        b.xres[cp][:, 512:1024], op=ALU.add).then_inc(b.v_yp, 1)

    if not _on("C2"):
        return
    for gg in range(NG):
        for tt in range(8):
            e.wait_ge(b.q_y, 16 * (gg * 32 + tt * 4 + 4))
            if gg == 0:
                for tc in range(4):
                    bb_ = tt * 4 + tc
                    e.wait_ge(b.t_b, 64 + bb_ + 1)
                    e.drain()
                    e.tensor_tensor(b.ysb[:, tc * D:tc * D + 512], b.PE[:, :],
                                    b.ysb[:, tc * D:tc * D + 512], op=ALU.add)
                    e.tensor_tensor(b.ysb[:, tc * D + 512:(tc + 1) * D],
                                    b.PF[:, :],
                                    b.ysb[:, tc * D + 512:(tc + 1) * D],
                                    op=ALU.add).then_inc(b.v_bb, 1)
            for ee in range(2):
                for tc in range(4):
                    du = ((gg * 8 + tt) * 2 + ee) * 4 + tc
                    dp = du % 2
                    pd = (b.PA, b.PA2) if dp == 0 else (b.PB, b.PB2)
                    t = tt // 2
                    r = 4 * (tt % 2) + tc
                    col = (t * 8 + r) * E + gg * 2 + ee
                    e.wait_ge(b.t_d, du + 1)
                    e.drain()
                    e.scalar_tensor_tensor(b.ysb[:, tc * D:tc * D + 512],
                                           pd[0][:, :], b.mtm[:, col:col + 1],
                                           b.ysb[:, tc * D:tc * D + 512],
                                           op0=ALU.mult, op1=ALU.add)
                    e.scalar_tensor_tensor(b.ysb[:, tc * D + 512:(tc + 1) * D],
                                           pd[1][:, :], b.mtm[:, col:col + 1],
                                           b.ysb[:, tc * D + 512:(tc + 1) * D],
                                           op0=ALU.mult,
                                           op1=ALU.add).then_inc(b.v_g, 1)


def _tensor(b, e):
    mm = e.matmul
    ku = vu = qu = ru = 0
    for tg in range(16):
        e.wait_ge(b.q_tl[tg % 2], 16 * 8 * (tg // 2 + 1))
        if tg == 0:
            e.wait_ge(b.q_wT, 16 * 24)
        hf = b.hifm[tg % 2]
        if _own(tg):
            for st in range(32):
                g512, hc = divmod(st, 8)
                tok = g512 * 512
                kp = ku % 2
                ps = b.PC if kp == 0 else b.PD
                if ku >= 2:
                    e.wait_ge(b.s_ke, ku - 1)
                for dc in range(8):
                    ins = mm(ps[:, :],
                             b.wkT[:, dc * D + hc * 128:dc * D + (hc + 1) * 128],
                             hf[:, dc * TG + tok:dc * TG + tok + 512],
                             start=(dc == 0), stop=(dc == 7))
                ins.then_inc(b.t_k, 1)
                ku += 1
                vp = vu % 2
                pv = b.PA if vp == 0 else b.PA2
                if vu >= 2:
                    e.wait_ge(b.s_ve, vu - 1)
                vtok = (st // 2) * 128
                h = st % 2
                for dc in range(8):
                    ins = mm(pv[:, :],
                             hf[:, dc * TG + vtok:dc * TG + vtok + 128],
                             b.wvT[:, dc * D + h * 512:dc * D + h * 512 + 512],
                             start=(dc == 0), stop=(dc == 7))
                ins.then_inc(b.t_v, 1)
                vu += 1
                if tg < 2:
                    qp = qu % 2
                    ps = b.PB if qp == 0 else b.PB2
                    if qu >= 2:
                        e.wait_ge(b.s_qe, qu - 1)
                    for dc in range(8):
                        ins = mm(ps[:, :],
                                 b.wqT[:, dc * D + hc * 128:dc * D + (hc + 1) * 128],
                                 hf[:, dc * TG + tok:dc * TG + tok + 512],
                                 start=(dc == 0), stop=(dc == 7))
                    ins.then_inc(b.t_q, 1)
                    qu += 1
        e.wait_ge(b.q_lo, 16 * 8 * (tg + 1))
        for g512 in range(4):
            tok = g512 * 512
            rp = ru % 2
            pr = b.PE if rp == 0 else b.PF
            if ru >= 2:
                e.wait_ge(b.s_re, ru - 1)
            for dc in range(8):
                mm(pr[0:E, :], b.wrhi[:, dc * E:(dc + 1) * E],
                   hf[:, dc * TG + tok:dc * TG + tok + 512],
                   start=(dc == 0), stop=False)
                mm(pr[0:E, :], b.wrlo[:, dc * E:(dc + 1) * E],
                   hf[:, dc * TG + tok:dc * TG + tok + 512],
                   start=False, stop=False)
                ins = mm(pr[0:E, :], b.wrhi[:, dc * E:(dc + 1) * E],
                         b.lofm[:, dc * TG + tok:dc * TG + tok + 512],
                         start=False, stop=(dc == 7))
            ins.then_inc(b.t_r, 1)
            ru += 1
        if _on("R") and 5 <= tg <= 15:
            if tg == 5:
                e.wait_ge(b.s_qe, 64)
            for gi in range(4):
                g = 4 * (tg - 5) + gi
                _stage_tr(b, e, g)

    if not _on("R"):
        return
    for g in range(44, 64):
        _stage_tr(b, e, g)
    for it in range(NITER):
        e.wait_ge(b.v_c, 3 + 2 * it)
        mm(b.PE[0:1, 0:E], b.onesc[:, :], b.part8[:, :], start=True,
           stop=True).then_inc(b.t_c2, 1)
        e.wait_ge(b.v_c, 4 + 2 * it)
        mm(b.PF[:, 0:E], b.ones1[0:1, :], b.theta[0:1, :], start=True,
           stop=True).then_inc(b.t_c2, 1)
    e.wait_ge(b.v_c, 3 + 2 * NITER)
    e.wait_ge(b.s_c, 2 * NITER + 1)
    mm(b.PF[:, 0:E], b.ones1[0:1, :], b.theta[0:1, :], start=True,
       stop=True).then_inc(b.t_c2, 1)

    if not _on("G"):
        return
    e.wait_ge(b.s_c, 2 * NITER + 2)
    for st in range(32):
        e.wait_ge(b.q_g, _qg_lstr(st))
        if st >= 1:
            e.wait_ge(b.s_fin, 2 * st)
        e.transpose(b.PF[:, 0:E], b.lstr[0:E, :],
                    b.identf[0:E, 0:E]).then_inc(b.t_b, 1)
        e.wait_ge(b.v_fin, st + 1)
        e.transpose(b.PE[0:E, 0:128], b.mtm[:, st * E:(st + 1) * E],
                    b.identf[:, :]).then_inc(b.t_b, 1)

    if not _on("B"):
        return
    for p in range(NPASS):
        for sb in range(NSB):
            g = p * NSB + sb
            par = g % 2
            if p == 0 and sb == 0 and _on("G"):
                e.wait_ge(b.s_fin, 64)
            e.wait_ge(b.q_qsb, 16 * 8 * (p + 1))
            e.wait_ge(b.q_kv[g % 2], 16 * 16 * (g // 2 + 1))
            for qg in range(NQG):
                pu = g * NQG + qg
                for kt in range(9):
                    if kt < 8:
                        su = pu * 8 + kt
                        sp = su % 2
                        if su >= 2:
                            e.wait_ge(b.s_e, su - 1)
                        ps = b.PC if sp == 0 else b.PD
                        for hc in range(8):
                            ins = mm(ps[:, 0:256],
                                     b.ksb[par][:, hc * SBK + kt * 128:
                                                hc * SBK + (kt + 1) * 128],
                                     b.qsb[:, hc * PQ + qg * 256:
                                           hc * PQ + (qg + 1) * 256],
                                     start=(hc == 0), stop=(hc == 7))
                        ins.then_inc(b.t_s, 1)
                    if kt >= 1:
                        ku2 = kt - 1
                        su = pu * 8 + ku2
                        sp = su % 2
                        e.wait_ge(b.s_e, su + 1)
                        if ku2 == 0:
                            e.wait_ge(b.v_m, p + 1 + pu)
                        stf = (ku2 == 0)
                        spf = (ku2 == 7)
                        for qc in range(2):
                            pt = b.pT[sp][:, qc * 128:(qc + 1) * 128]
                            po = (b.PA, b.PA2) if qc == 0 else (b.PB, b.PB2)
                            mm(po[0][:, :], pt,
                               b.vsb[par][:, ku2 * SBK:ku2 * SBK + 512],
                               start=stf, stop=spf)
                            mm(po[1][:, :], pt,
                               b.vsb[par][:, ku2 * SBK + 512:(ku2 + 1) * SBK],
                               start=stf, stop=spf)
                            pl = b.PE if qc == 0 else b.PF
                            ins = mm(pl[:, qg:qg + 1], pt,
                                     b.onesb[:, :], start=stf, stop=spf)
                ins.then_inc(b.t_pv, 1)

    if not _on("C1"):
        return
    e.wait_ge(b.q_wT, 16 * 32)
    for cu in range(32):
        t, ddc = divmod(cu, 8)
        cp = cu % 2
        pc = (b.PA, b.PA2) if cp == 0 else (b.PB, b.PB2)
        e.wait_ge(b.q_on2, 16 * 8 * (t + 1))
        if cu >= 2:
            e.wait_ge(b.v_yp, cu - 1)
        for jc in range(8):
            st = b.onld[:, jc * D + ddc * 128:jc * D + (ddc + 1) * 128]
            mm(pc[0][:, :], st, b.woT[:, jc * D:jc * D + 512],
               start=(jc == 0), stop=(jc == 7))
            ins = mm(pc[1][:, :], st, b.woT[:, jc * D + 512:(jc + 1) * D],
                     start=(jc == 0), stop=(jc == 7))
        ins.then_inc(b.t_c, 1)

    if not _on("C2"):
        return
    uu = 0
    du = 0
    e.wait_ge(b.q_c2c, 16 * 9)
    for gg in range(NG):
        for tt in range(8):
            xp = tt % 2
            gt = gg * 8 + tt
            e.wait_ge(b.q_xm[gt % 2], 16 * 32 * (gt // 2 + 1))
            if gg == 0:
                e.wait_ge(b.s_fin, 64)
                t = tt // 2
                for tc in range(4):
                    bb_ = tt * 4 + tc
                    if bb_ >= 1:
                        e.wait_ge(b.v_bb, bb_)
                    r = 4 * (tt % 2) + tc
                    st8 = (t * 8 + r) * 128
                    mm(b.PE[:, :], b.mfm[:, st8:st8 + 128], b.bdb[:, 0:512],
                       start=True, stop=True)
                    mm(b.PF[:, :], b.mfm[:, st8:st8 + 128], b.bdb[:, 512:1024],
                       start=True, stop=True).then_inc(b.t_b, 1)
            if gg == 1 and tt == 0:
                e.wait_ge(b.v_bb, 32)
            for fi, (f0, fw) in enumerate(FCH):
                if tt == 0 and fi == 0:
                    e.wait_ge(b.q_wq[0], 16 * (gg + 1))
                ps = _c2_up_bank(b, gg, uu)
                dep = 2 if gg == 0 else 4
                if uu >= dep:
                    e.wait_ge(b.s_u, uu - dep + 1)
                for dc in range(8):
                    ins = mm(ps[0:fw, :],
                             b.wupsb[0:128, dc * FF + f0:dc * FF + f0 + fw],
                             b.xmv[xp][:, dc * 512:(dc + 1) * 512],
                             start=(dc == 0), stop=(dc == 7))
                ins.then_inc(b.t_u, 1)
                uu += 1
            e.wait_ge(b.s_u, uu)
            for ee in range(2):
                if tt == 0:
                    e.wait_ge(b.q_wt if ee == 0 else b.q_wd,
                              16 * 2 * (gg + 1))
                for tc in range(4):
                    dp = du % 2
                    pd = (b.PA, b.PA2) if dp == 0 else (b.PB, b.PB2)
                    if du >= 2:
                        e.wait_ge(b.v_g, du - 1)
                    for fi, (f0, fw) in enumerate(FCH):
                        st = b.mid[0:fw, fi * 512 + tc * 128:
                                   fi * 512 + (tc + 1) * 128]
                        mm(pd[0][:, :], st,
                           b.wdsb[ee][0:fw, fi * D:fi * D + 512],
                           start=(fi == 0), stop=(fi == NFC - 1))
                        ins = mm(pd[1][:, :], st,
                                 b.wdsb[ee][0:fw, fi * D + 512:(fi + 1) * D],
                                 start=(fi == 0), stop=(fi == NFC - 1))
                    ins.then_inc(b.t_d, 1)
                    du += 1


def _gpsimd(b, e):
    e.iota(b.colio[:, :], pattern=[[1, 128]], base=0, channel_multiplier=0,
           allow_small_or_imprecise_dtypes=True).then_inc(b.g_c, 1)
    e.iota(b.rowio[:, :], pattern=[[1, 1]], base=0, channel_multiplier=1,
           allow_small_or_imprecise_dtypes=True).then_inc(b.g_c, 1)
    e.dma_start(b.wrf[:, :].rearrange("p (c e) -> p c e", e=E),
                b.wr[:, :].rearrange("(c p) e -> c p e", p=128)
                .rearrange("c p e -> p c e")).then_inc(b.q_g, 16)
    e.dma_start(b.brsb[:, :], b.br[:, :]).then_inc(b.q_g, 16)
    for j, w in enumerate([b.wq, b.wk, b.wv, b.wo]):
        e.dma_start(b.wld[:, :].rearrange("p (c d) -> p c d", d=D),
                    w[:, :].rearrange("(c p) d -> c p d", p=128)
                    .rearrange("c p d -> p c d")).then_inc(b.q_wl, 16)
        e.wait_ge(b.q_wl, 16 * (j + 1))
        e.dma_start(b.wnat[j, :, :].rearrange("(c p) d -> c p d", p=128)
                    .rearrange("c p d -> p c d"),
                    b.wld[:, :].rearrange("p (c d) -> p c d", d=D)
                    ).then_inc(b.q_w, 16)
        e.wait_ge(b.q_w, 16 * (j + 1))
    xn_at = {}
    for u in range(256):
        ptg, j = divmod(u, 32)
        tile = 16 * (ptg + 2) + 12 + j // 4 if j < 16 else \
            16 * (ptg + 3) + j // 4 - 4
        xn_at.setdefault(tile + 1, []).append(u)
    for k in range(260):
        if k <= 255:
            if k >= 5:
                e.wait_ge(b.g_lo, k - 4)
            e.dma_start(b.xr[k % 5][:, :], _xsrc(b, k)
                        ).then_inc(b.q_xr[k % 5], 16)
        for u in xn_at.get(k, ()):
            e.wait_ge(b.v_xe, u + 1)
            tg8 = u // 32
            t, r = divmod(u % 32, 8)
            e.dma_start(b.xnic[t, r, tg8, :, :], b.xist[u % 2][:, :]
                        ).then_inc(b.q_xs[u % 2], 16)
        j = k - 4
        if 0 <= j <= 255:
            if j % 16 == 0 and j >= 16:
                e.wait_ge(b.s_ts, j // 16)
            e.wait_ge(b.g_lo, j + 1)
            pp = j % 2
            dst = j * 128
            e.dma_start(b.xnhi[dst:dst + 128, :],
                        b.hitm[pp][:, :]).then_inc(b.q_st[pp], 16)
            e.dma_start(b.xnlo[dst:dst + 128, :],
                        b.lotm[pp][:, :]).then_inc(b.q_st[pp], 16)

    if not _on("G"):
        return
    e.wait_ge(b.s_c, 2 * NITER + 2)
    e.wait_ge(b.q_rd, 16 * 64)
    for st in range(32):
        t, r = divmod(st, 8)
        off = 16 * r + t
        if st >= 1:
            e.wait_ge(b.t_b, 2 * st - 1)
        e.dma_start(b.lstr[:, :], b.logit[:, off:off + S:128],
                    allow_slow_non_contiguous=True).then_inc(b.q_g, 16)

    if not _on("C1"):
        return
    e.wait_ge(b.t_r, 64)
    e.wait_ge(b.q_on[0], 16 * 16)
    e.wait_ge(b.q_on[1], 16 * 16)
    for cu in range(32):
        t, ddc = divmod(cu, 8)
        row0 = 16 * (ddc * 128) + t
        if cu >= 1:
            e.wait_ge(b.v_yp, cu)
        e.dma_start(b.xres[cu % 2][:, :],
                    b.x_own[row0:row0 + 2033:16, :]).then_inc(b.q_g, 16)

    if not _on("C2"):
        return
    e.wait_ge(b.t_c, 32)
    e.wait_ge(b.v_yp, 32)
    e.wait_ge(b.q_yp, 16 * 32)
    e.wait_ge(b.s_fin, 64)
    e.dma_start(b.bdb[:, :], b.bdn[:, :]).then_inc(b.q_c2c, 16)
    for g in range(NG):
        e.dma_start(b.bupsb[:, g * NFC:g * NFC + 21],
                    b.bup[g, 0:2688].rearrange("(c p) -> c p", p=128)
                    .rearrange("c p -> p c"),
                    allow_slow_non_contiguous=True).then_inc(b.q_c2c, 16)
        e.dma_start(b.bupsb[0:42, g * NFC + 21:g * NFC + 22],
                    b.bup[g, 2688:2730].rearrange("(f o) -> f o", o=1)
                    ).then_inc(b.q_c2c, 16)
    for gg in range(NG):
        if gg >= 1:
            e.wait_ge(b.t_u, gg * 8 * NFC)
            e.wait_ge(b.t_d, gg * 64)
        e.dma_start(b.wupsb[:, :].rearrange("p (c f) -> p c f", f=FF),
                    b.wup[gg, :, :].rearrange("(c p) f -> c p f", p=128)
                    .rearrange("c p f -> p c f")).then_inc(b.q_wq[0], 16)
        for ee in range(2):
            ex = 2 * gg + ee
            sw = b.q_wt if ee == 0 else b.q_wd
            e.dma_start(b.wdsb[ee][:, 0:21 * D].rearrange("p (c d) -> p c d", d=D),
                        b.wdn[ex, 0:2688, :].rearrange("(c p) d -> c p d", p=128)
                        .rearrange("c p d -> p c d")).then_inc(sw, 16)
            e.dma_start(b.wdsb[ee][0:42, 21 * D:22 * D],
                        b.wdn[ex, 2688:2730, :]).then_inc(sw, 16)
        for tt in range(8):
            xp = tt % 2
            if gg * 8 + tt >= 2:
                e.wait_ge(b.t_u, (gg * 8 + tt - 1) * NFC)
            t = tt // 2
            for dc in range(8):
                for rr in range(4):
                    r = 4 * (tt % 2) + rr
                    e.dma_start(
                        b.xmv[xp][:, dc * 512 + rr * 128:dc * 512 + (rr + 1) * 128]
                        .rearrange("p (g h) -> p g h", h=16),
                        b.xnic[t, r, :, :, dc * 16:(dc + 1) * 16]
                        .rearrange("g p h -> p g h")
                        ).then_inc(b.q_xm[(gg * 8 + tt) % 2], 16)


def kernel(L0_attn_q, L0_attn_wq, L0_attn_wk, L0_attn_wv, L0_attn_wo,
           L0_ffn_Wr, L0_ffn_br, L0_ffn_W_up, L0_ffn_b_up, L0_ffn_W_down,
           L0_ffn_b_down):
    x = np.asarray(L0_attn_q)
    nc = build()
    in_maps = []
    for c in range(8):
        bb, a = c // 4, c % 4
        xb = np.ascontiguousarray(
            np.roll(np.roll(x[bb].reshape(1024, 16, D), -256 * a, axis=0),
                    -4 * a, axis=1).reshape(S, D))
        wv_p = np.ascontiguousarray(np.roll(np.asarray(L0_attn_wv), -256 * a,
                                            axis=0))
        wo_p = np.ascontiguousarray(
            np.roll(np.asarray(L0_attn_wo).reshape(D, 64, 16), -4 * a,
                    axis=2).reshape(D, D))
        in_maps.append({
            "x_own": xb, "x_oth": np.ascontiguousarray(x[1 - bb]),
            "wq": np.asarray(L0_attn_wq), "wk": np.asarray(L0_attn_wk),
            "wv": wv_p, "wo": wo_p,
            "wr": np.asarray(L0_ffn_Wr),
            "br": np.asarray(L0_ffn_br).reshape(E, 1),
            "wup": np.asarray(L0_ffn_W_up), "bup": np.asarray(L0_ffn_b_up),
            "wdn": np.asarray(L0_ffn_W_down), "bdn": np.asarray(L0_ffn_b_down),
        })
    res = run_bass_kernel_spmd(nc, in_maps, core_ids=list(range(8)))
    out = np.empty((2, S, D), dtype=np.float32)
    for c in range(8):
        bb, a = c // 4, c % 4
        yc = np.asarray(res.results[c]["y"])
        tmp = yc.reshape(4, 8, 128, D).transpose(0, 2, 1, 3).reshape(4, 1024, D)
        part = np.moveaxis(tmp, 0, 1)
        rolled = np.roll(part, 256 * a, axis=0)
        out[bb].reshape(1024, 16, D)[:, 4 * a:4 * a + 4, :] = rolled
    return out.reshape(2, S, D)


# version 13
# speedup vs baseline: 1.0879x; 1.0879x over previous
import numpy as np
import ml_dtypes
import concourse.bass as bass
import concourse.mybir as mybir
from concourse.bass_utils import run_bass_kernel_spmd

F32 = mybir.dt.float32
BF16 = mybir.dt.bfloat16
AF = mybir.ActivationFunctionType
ALU = mybir.AluOpType
AX = mybir.AxisListType
BF = ml_dtypes.bfloat16

D = 1024
S = 16384
NTOK = 2 * S
QS = 4096
E = 8
NG = 4
FF = 2730
FCH = [(i * 128, 128) for i in range(21)] + [(2688, 42)]
NFC = len(FCH)
TG = 2048
NCH = NTOK // 128
SBK = 1024
NSB = S // SBK
NPASS = 2
PQ = QS // NPASS
NQG = PQ // 256
SHIFT = 24.0
RSHIFT = 12.0
NITER = 29
KSEL = 4096.0
EPS = 1e-5

UPTO = "C2"
DEBUG = False

_ORD = ["A", "R", "G", "B", "C1", "C2"]


def _on(ph):
    return _ORD.index(ph) <= _ORD.index(UPTO)


class Bx:
    pass


def build():
    nc = bass.Bass("TRN2", target_bir_lowering=False)
    b = Bx()
    b.nc = nc

    b.x_own = nc.dram_tensor("x_own", [S, D], F32, kind="ExternalInput").ap()
    b.x_oth = nc.dram_tensor("x_oth", [S, D], F32, kind="ExternalInput").ap()
    b.wq = nc.dram_tensor("wq", [D, D], F32, kind="ExternalInput").ap()
    b.wk = nc.dram_tensor("wk", [D, D], F32, kind="ExternalInput").ap()
    b.wv = nc.dram_tensor("wv", [D, D], F32, kind="ExternalInput").ap()
    b.wo = nc.dram_tensor("wo", [D, D], F32, kind="ExternalInput").ap()
    b.wr = nc.dram_tensor("wr", [D, E], F32, kind="ExternalInput").ap()
    b.br = nc.dram_tensor("br", [E, 1], F32, kind="ExternalInput").ap()
    b.wup = nc.dram_tensor("wup", [NG, D, FF], F32, kind="ExternalInput").ap()
    b.bup = nc.dram_tensor("bup", [NG, FF], F32, kind="ExternalInput").ap()
    b.wdn = nc.dram_tensor("wdn", [E, FF, D], F32, kind="ExternalInput").ap()
    b.bdn = nc.dram_tensor("bdn", [E, D], F32, kind="ExternalInput").ap()
    b.y = nc.dram_tensor("y", [QS, D], F32, kind="ExternalOutput").ap()

    dt = nc.dram_tensor
    b.wnat = dt("wnat", [4, D, D], BF16, kind="Internal").ap()
    b.xnhi = dt("xnhi", [NTOK, D], BF16, kind="Internal").ap()
    b.xnlo = dt("xnlo", [NTOK, D], BF16, kind="Internal").ap()
    b.kfm = dt("kfm", [8, 128, S], BF16, kind="Internal").ap()
    b.vtm = dt("vtm", [S, D], BF16, kind="Internal").ap()
    b.qfm = dt("qfm", [8, 128, QS], BF16, kind="Internal").ap()
    b.outn = dt("outn", [QS, D], BF16, kind="Internal").ap()
    b.ypart = dt("ypart", [QS, D], F32, kind="Internal").ap()
    b.logit = dt("logit", [E, NTOK], F32, kind="Internal").ap()
    b.xnic = dt("xnic", [4, 8, 8, 128, 128], BF16, kind="Internal").ap()
    if DEBUG:
        b.dbg_kfm = dt("dbg_kfm", [8, 128, S], BF16, kind="ExternalOutput").ap()
        b.dbg_vtm = dt("dbg_vtm", [S, D], BF16, kind="ExternalOutput").ap()
        b.dbg_qfm = dt("dbg_qfm", [8, 128, QS], BF16, kind="ExternalOutput").ap()
        b.dbg_logit = dt("dbg_logit", [E, NTOK], F32, kind="ExternalOutput").ap()
        b.dbg_xnhi = dt("dbg_xnhi", [NTOK, D], BF16, kind="ExternalOutput").ap()
        b.dbg_xnic = dt("dbg_xnic", [4, 8, 8, 128, 128], BF16, kind="ExternalOutput").ap()
        b.dbg_thr = dt("dbg_thr", [128, E], F32, kind="ExternalOutput").ap()
        b.dbg_sm = dt("dbg_sm", [128, NCH * E], F32, kind="ExternalOutput").ap()
        b.dbg_outn = dt("dbg_outn", [QS, D], BF16, kind="ExternalOutput").ap()
        b.dbg_ypart = dt("dbg_ypart", [QS, D], F32, kind="ExternalOutput").ap()
        b.dbg_mtm = dt("dbg_mtm", [128, 32 * E], F32, kind="ExternalOutput").ap()

    POS = [16384 + 1408]

    def alloc(name, shape, dtyp, at=None):
        if at is not None:
            POS[0] = at
        h = nc.alloc_sbuf_tensor_at(name, shape, dtyp, offset=POS[0])
        n = int(np.prod(shape[1:])) * (4 if dtyp == F32 else 2)
        POS[0] = (POS[0] + n + 31) & ~31
        return h

    b.identf = alloc("identf", [128, 128], F32)
    b.colio = alloc("colio", [128, 128], F32)
    b.rowio = alloc("rowio", [128, 1], F32)
    b.wrhi = alloc("wrhi", [128, 8 * E], BF16)
    b.wrlo = alloc("wrlo", [128, 8 * E], BF16)
    b.wrf = alloc("wrf", [128, 8 * E], F32)
    b.brsb = alloc("brsb", [E, 1], F32)
    b.ltm = alloc("ltm", [128, NCH * E], F32)
    b.theta = alloc("theta", [1, E], F32)
    b.threp = alloc("threp", [128, E], F32)
    b.mtm = alloc("mtm", [128, 32 * E], F32)
    OFF_MFM = POS[0]
    b.mfm = alloc("mfm", [E, QS], BF16)
    b.cnt = alloc("cnt", [1, E], F32)
    b.part8 = alloc("part8", [128, E], F32)
    b.g8 = alloc("g8", [128, E], F32)
    b.ones1 = alloc("ones1", [1, 128], F32)
    b.onesc = alloc("onesc", [128, 1], F32)
    b.onesb = alloc("onesb", [128, 1], BF16)
    b.bdb = alloc("bdb", [E, D], BF16)
    b.ustr = alloc("ustr", [128, E], F32)
    b.cm12 = alloc("cm12", [128, 1], F32)
    b.cm24 = alloc("cm24", [128, 1], F32)
    b.c0125 = alloc("c0125", [128, 1], F32)
    b.z8 = alloc("z8", [128, 2], F32)
    OFF_LSTR = POS[0]
    b.lstr = alloc("lstr", [E, 128], F32)
    PERS = POS[0]

    b.wqT = alloc("wqT", [128, 8 * D], BF16, at=PERS)
    b.wkT = alloc("wkT", [128, 8 * D], BF16)
    b.wvT = alloc("wvT", [128, 8 * D], BF16)
    b.hifm = [alloc("hifm0", [128, 8 * TG], BF16),
              alloc("hifm1", [128, 8 * TG], BF16)]
    OFF_LOFM = POS[0]
    b.lofm = alloc("lofm", [128, 8 * TG], BF16)
    b.xr = [alloc("xr%d" % i, [128, D], F32) for i in range(5)]
    b.hitm = [alloc("hitm0", [128, D], BF16), alloc("hitm1", [128, D], BF16)]
    b.lotm = [alloc("lotm0", [128, D], BF16), alloc("lotm1", [128, D], BF16)]
    b.bn12 = [alloc("bn12_0", [128, 12], F32), alloc("bn12_1", [128, 12], F32)]
    b.statm = [alloc("statm0", [128, 96], F32), alloc("statm1", [128, 96], F32)]
    b.kst = [alloc("kst0", [128, 512], BF16), alloc("kst1", [128, 512], BF16)]
    b.vst = [alloc("vst0", [128, D], BF16), alloc("vst1", [128, D], BF16)]
    b.qst = [alloc("qst0", [128, 512], BF16), alloc("qst1", [128, 512], BF16)]
    b.xist = [alloc("xist0", [128, 128], BF16), alloc("xist1", [128, 128], BF16)]
    endA = POS[0]
    b.wld = alloc("wld", [128, 8 * D], BF16, at=OFF_LOFM)

    b.smtm = alloc("smtm", [128, NCH * E], F32, at=PERS)
    b.ztm = alloc("ztm", [128, NCH], F32)
    b.ind = alloc("ind", [128, NCH], F32)
    b.qsb = alloc("qsb", [128, 8 * PQ], BF16)
    b.osb = alloc("osb", [128, 16 * D], F32)
    b.ksb = [alloc("ksb0", [128, 8 * SBK], BF16),
             alloc("ksb1", [128, 8 * SBK], BF16)]
    b.vsb = [alloc("vsb0", [128, 8 * SBK], BF16),
             alloc("vsb1", [128, 8 * SBK], BF16)]
    b.pT = [alloc("pT0", [128, 256], BF16), alloc("pT1", [128, 256], BF16)]
    b.lsb = alloc("lsb", [128, 16], F32)
    b.lrec = alloc("lrec", [128, 16], F32)
    b.onst = [alloc("onst0", [128, D], BF16), alloc("onst1", [128, D], BF16)]
    endB = POS[0]

    b.woT = alloc("woT", [128, 8 * D], BF16, at=PERS)
    b.onld = alloc("onld", [128, 8 * D], BF16)
    b.xres = [alloc("xres0", [128, D], F32), alloc("xres1", [128, D], F32)]
    b.ypst = [alloc("ypst0", [128, D], F32), alloc("ypst1", [128, D], F32)]
    endC1 = POS[0]

    b.wupsb = alloc("wupsb", [128, 8 * FF], BF16, at=PERS)
    b.wdsb = [alloc("wdsb0", [128, NFC * D], BF16),
              alloc("wdsb1", [128, NFC * D], BF16)]
    b.mid = alloc("mid", [128, NFC * 512], BF16)
    b.xmv = [alloc("xmv0", [128, 8 * 512], BF16),
             alloc("xmv1", [128, 8 * 512], BF16)]
    b.ysb = alloc("ysb", [128, 4 * D], F32)
    endC2 = POS[0]
    b.rst = [alloc("rst0", [E, 512], F32, at=OFF_MFM),
             alloc("rst1", [E, 512], F32)]
    b.rls = [alloc("rls0", [E, 512], F32),
             alloc("rls1", [E, 512], F32)]
    b.bupsb = alloc("bupsb", [128, NG * NFC], F32, at=OFF_LSTR)

    hwm = max(endA, endB, endC1, endC2)
    assert hwm <= 229344, f"SBUF over budget: {hwm}"

    b.PA = nc.alloc_psum_tensor("PA", [128, 512], F32)
    b.PA2 = nc.alloc_psum_tensor("PA2", [128, 512], F32)
    b.PB = nc.alloc_psum_tensor("PB", [128, 512], F32)
    b.PB2 = nc.alloc_psum_tensor("PB2", [128, 512], F32)
    b.PC = nc.alloc_psum_tensor("PC", [128, 512], F32)
    b.PD = nc.alloc_psum_tensor("PD", [128, 512], F32)
    b.PE = nc.alloc_psum_tensor("PE", [128, 512], F32)
    b.PF = nc.alloc_psum_tensor("PF", [128, 512], F32)

    from contextlib import ExitStack
    with ExitStack() as ctx:
        for name in ["v_bn", "s_sq", "v_bs", "v_xn", "s_hi", "g_lo", "s_ts",
                     "t_k", "s_ke", "t_v", "s_ve",
                     "t_q", "s_qe", "t_r", "s_re", "q_lt", "t_lt", "s_lt",
                     "q_rd", "t_a", "v_xe", "q_w", "q_wl", "g_c", "q_g",
                     "v_c", "s_c", "t_c2", "q_qsb", "t_sb", "t_s", "s_e",
                     "t_pv", "v_m", "v_r", "s_n", "t_c", "v_yp",
                     "q_yp", "q_y", "t_u", "s_u", "t_d", "v_g", "t_b",
                     "v_fin", "s_fin", "v_bb", "q_dbg", "q_wT", "q_lo",
                     "q_on2", "q_c2c", "q_wt"]:
            setattr(b, name, ctx.enter_context(nc.semaphore(name)))
        for name in ["q_st", "q_tl", "q_ke", "q_ve", "q_qe",
                     "q_xs", "q_kv", "q_on", "q_xm"]:
            setattr(b, name, [ctx.enter_context(nc.semaphore(name + "0")),
                              ctx.enter_context(nc.semaphore(name + "1"))])
        b.q_xr = [ctx.enter_context(nc.semaphore("q_xr%d" % i))
                  for i in range(5)]
        b.q_ls = [ctx.enter_context(nc.semaphore("q_ls0")),
                  ctx.enter_context(nc.semaphore("q_ls1"))]
        b.q_wq = [ctx.enter_context(nc.semaphore("q_wq%d" % i))
                  for i in range(4)]
        b.q_wd = ctx.enter_context(nc.semaphore("q_wd"))
        blk = ctx.enter_context(nc.Block())

        @blk.sync
        def _(e):
            _sync(b, e)

        @blk.scalar
        def _(e):
            _scalar(b, e)

        @blk.vector
        def _(e):
            _vector(b, e)

        @blk.tensor
        def _(e):
            _tensor(b, e)

        @blk.gpsimd
        def _(e):
            _gpsimd(b, e)
    return nc


def _xsrc(b, n):
    tg, i = divmod(n, 16)
    base = (tg % 8) * TG + i * 128
    src = b.x_own if tg < 8 else b.x_oth
    return src[base:base + 128, :]


def _own(tg):
    return tg < 8


def _qkv_b(p, sb):
    return 16 * (p * (8 + 16 * NSB) + 8 + (sb + 1) * 16)


def _qkv_c1(t):
    return 16 * (2 * (8 + 16 * NSB) + 8 * t + 8)


def _qg_xres(cu):
    return 16 * (2 + 32 + cu + 1)


def _qg_lstr(st):
    return 16 * (2 + st + 1)


def _tl_hi(b, e, htg, c, first):
    if first:
        if htg >= 2:
            e.wait_ge(b.t_r, 4 * (htg - 1))
        e.wait_ge(b.q_st[0], 32 * 8 * (htg + 1))
        e.wait_ge(b.q_st[1], 32 * 8 * (htg + 1))
        e.nop().then_inc(b.s_ts, 1)
        if htg >= 2 and _own(htg - 2):
            e.wait_ge(b.v_xe, 32 * (htg - 1))
    e.dma_start(b.hifm[htg % 2][:, c * TG:(c + 1) * TG],
                b.xnhi[htg * TG:(htg + 1) * TG, c * 128:(c + 1) * 128],
                transpose=True).then_inc(b.q_tl[htg % 2], 16)


def _tl_lo(b, e, ltg, c, first):
    if first:
        if ltg >= 1:
            e.wait_ge(b.t_r, 4 * ltg)
        else:
            e.wait_ge(b.q_w, 16 * 4)
        e.wait_ge(b.q_st[0], 32 * 8 * (ltg + 1))
        e.wait_ge(b.q_st[1], 32 * 8 * (ltg + 1))
    e.dma_start(b.lofm[:, c * TG:(c + 1) * TG],
                b.xnlo[ltg * TG:(ltg + 1) * TG, c * 128:(c + 1) * 128],
                transpose=True).then_inc(b.q_lo, 16)


def _lt_evac(b, e, g):
    e.wait_ge(b.t_lt, g + 1)
    pr = b.PB if g % 2 == 0 else b.PB2
    e.tensor_copy(b.ltm[:, g * 4 * E:(g + 1) * 4 * E],
                  pr[:, 0:32]).then_inc(b.s_lt, 1)


def _stage_tr(b, e, g):
    e.wait_ge(b.q_ls[g % 2], 16 * (g // 2 + 1))
    pr = b.PB if g % 2 == 0 else b.PB2
    if g >= 2:
        e.wait_ge(b.s_lt, g - 1)
    for j in range(4):
        ins = e.transpose(pr[:, j * E:(j + 1) * E],
                          b.rls[g % 2][0:E, j * 128:(j + 1) * 128],
                          b.identf[0:E, 0:E])
    ins.then_inc(b.t_lt, 1)


def _c2_up_bank(b, gg, uu):
    if gg == 0:
        return (b.PC, b.PD)[uu % 2]
    return (b.PC, b.PD, b.PE, b.PF)[uu % 4]


def _qg_c2w(gg):
    return 16 * (2 + 32 + 32 + 1 + 8 + gg * 37 + 5)


def _qg_c2x(gg, tt):
    return 16 * (2 + 32 + 32 + 1 + 8 + gg * 37 + 5 + (tt + 1) * 32)


def _sync(b, e):
    for j, wT in enumerate([b.wqT, b.wkT, b.wvT]):
        e.wait_ge(b.q_w, 16 * (j + 1))
        for dc in range(8):
            e.dma_start(wT[:, dc * D:(dc + 1) * D],
                        b.wnat[j, :, dc * 128:(dc + 1) * 128],
                        transpose=True).then_inc(b.q_wT, 16)

    def stage_load(g):
        e.wait_ge(b.q_rd, 16 * min(64, g + 2))
        if g >= 2:
            e.wait_ge(b.t_lt, g - 1)
        e.dma_start(b.rls[g % 2][:, :],
                    b.logit[:, g * 512:(g + 1) * 512]
                    ).then_inc(b.q_ls[g % 2], 16)

    for tau in range(18):
        if 0 <= tau - 2 <= 15:
            for c in range(8):
                _tl_hi(b, e, tau - 2, c, c == 0)
            for c in range(8):
                _tl_lo(b, e, tau - 2, c, c == 0)
        if _on("R") and 0 <= tau - 8 <= 9:
            for gi in range(4):
                stage_load(4 * (tau - 8) + gi)
    if _on("R"):
        for g in range(40, 64):
            stage_load(g)

    if not _on("B"):
        e.wait_ge(b.q_wT, 16 * 24)
        e.wait_ge(b.q_lo, 16 * 128)
        for par in range(2):
            e.wait_ge(b.q_tl[par], 16 * 64)
            e.wait_ge(b.q_ke[par], 16 * 128)
            e.wait_ge(b.q_ve[par], 16 * 128)
            e.wait_ge(b.q_qe[par], 16 * 32)
            e.wait_ge(b.q_xs[par], 16 * 128)
        e.wait_ge(b.q_rd, 16 * 64)
        if _on("R"):
            e.wait_ge(b.s_c, 2 * NITER + 2)
        _dbg_taps(b, e)
        return

    for p in range(NPASS):
        if p == 0:
            e.wait_ge(b.t_r, 64)
            e.wait_ge(b.q_xs[0], 16 * 128)
            e.wait_ge(b.q_xs[1], 16 * 128)
            for par in range(2):
                e.wait_ge(b.q_ke[par], 16 * 128)
                e.wait_ge(b.q_ve[par], 16 * 128)
                e.wait_ge(b.q_qe[par], 16 * 32)
            e.wait_ge(b.s_c, 2 * NITER + 2)
        else:
            e.wait_ge(b.t_pv, 8 * NSB)
        for dc in range(8):
            e.dma_start(b.qsb[:, dc * PQ:(dc + 1) * PQ],
                        b.qfm[dc, :, p * PQ:(p + 1) * PQ]).then_inc(b.q_qsb, 16)
        for sb in range(NSB):
            g = p * NSB + sb
            par = g % 2
            if g >= 2:
                e.wait_ge(b.t_pv, 8 * (g - 1))
            for hc in range(8):
                e.dma_start(b.ksb[par][:, hc * SBK:(hc + 1) * SBK],
                            b.kfm[hc, :, sb * SBK:(sb + 1) * SBK]
                            ).then_inc(b.q_kv[par], 16)
            for tc in range(8):
                row = sb * SBK + tc * 128
                e.dma_start(b.vsb[par][:, tc * SBK:(tc + 1) * SBK],
                            b.vtm[row:row + 128, :]).then_inc(b.q_kv[par], 16)

    if not _on("C1"):
        e.wait_ge(b.q_on[0], 16 * 16)
        e.wait_ge(b.q_on[1], 16 * 16)
        _dbg_taps(b, e)
        return

    e.wait_ge(b.q_on[0], 16 * 16)
    e.wait_ge(b.q_on[1], 16 * 16)
    e.wait_ge(b.q_w, 16 * 4)
    for dc in range(8):
        e.dma_start(b.woT[:, dc * D:(dc + 1) * D],
                    b.wnat[3, :, dc * 128:(dc + 1) * 128],
                    transpose=True).then_inc(b.q_wT, 16)
    for t in range(4):
        if t >= 1:
            e.wait_ge(b.t_c, 8 * t)
        for jc in range(8):
            row = t * 1024 + jc * 128
            e.dma_start(b.onld[:, jc * D:(jc + 1) * D],
                        b.outn[row:row + 128, :]).then_inc(b.q_on2, 16)

    if not _on("C2"):
        e.wait_ge(b.q_yp, 16 * 32)
        if _on("G"):
            e.wait_ge(b.s_fin, 64)
        _dbg_taps(b, e)
        return

    for gg in range(NG):
        for tt in range(8):
            t = tt // 2
            e.wait_ge(b.q_yp, 16 * (32 + gg * 32 + tt * 4))
            for tc in range(4):
                r = 4 * (tt % 2) + tc
                row0 = t * 1024 + r
                e.dma_start(b.ysb[:, tc * D:(tc + 1) * D],
                            b.ypart[row0:row0 + 1017:8, :]).then_inc(b.q_y, 16)
    e.wait_ge(b.q_yp, 16 * (32 + NG * 32))
    _dbg_taps(b, e)


def _dbg_taps(b, e):
    if not DEBUG:
        return
    n = 0
    pairs = []
    if _on("A"):
        pairs += [(b.dbg_kfm, b.kfm), (b.dbg_vtm, b.vtm), (b.dbg_qfm, b.qfm),
                  (b.dbg_logit, b.logit), (b.dbg_xnhi, b.xnhi),
                  (b.dbg_xnic, b.xnic)]
    if _on("B"):
        pairs += [(b.dbg_outn, b.outn)]
    if _on("C1"):
        pairs += [(b.dbg_ypart, b.ypart)]
    for dst, src in pairs:
        e.dma_start(dst, src).then_inc(b.q_dbg, 16)
        n += 1
    if _on("R"):
        e.dma_start(b.dbg_thr, b.threp[:, :]).then_inc(b.q_dbg, 16)
        n += 1
        if not _on("C1"):
            e.dma_start(b.dbg_sm, b.smtm[:, :]).then_inc(b.q_dbg, 16)
            n += 1
    if _on("G"):
        e.dma_start(b.dbg_mtm, b.mtm[:, :]).then_inc(b.q_dbg, 16)
        n += 1
    e.wait_ge(b.q_dbg, 16 * n)


def _scalar(b, e):
    ku = vu = qu = ru = 0

    def k_unit(utg):
        nonlocal ku
        kp = ku % 2
        e.wait_ge(b.t_k, ku + 1)
        if ku >= 2:
            e.wait_ge(b.q_ke[kp], 16 * (ku // 2))
        ps = b.PC if kp == 0 else b.PD
        e.activation(b.kst[kp][:, :], ps[:, :], AF.Copy)
        e.drain()
        e.nop().then_inc(b.s_ke, 1)
        g512, hc = divmod(ku % 32, 8)
        e.dma_start(b.kfm[hc, :, utg * TG + g512 * 512:
                          utg * TG + (g512 + 1) * 512],
                    b.kst[kp][:, :]).then_inc(b.q_ke[kp], 16)
        ku += 1

    def v_unit(utg):
        nonlocal vu
        vp = vu % 2
        e.wait_ge(b.t_v, vu + 1)
        if vu >= 2:
            e.wait_ge(b.q_ve[vp], 16 * (vu // 2))
        ps = b.PA if vp == 0 else b.PA2
        e.activation(b.vst[vp][:, 0:512], ps[:, :], AF.Copy)
        e.drain()
        e.nop().then_inc(b.s_ve, 1)
        idx = vu % 32
        row = utg * TG + (idx // 2) * 128
        h = idx % 2
        e.dma_start(b.vtm[row:row + 128, h * 512:h * 512 + 512],
                    b.vst[vp][:, 0:512]).then_inc(b.q_ve[vp], 16)
        vu += 1

    def q_unit(utg):
        nonlocal qu
        qp = qu % 2
        e.wait_ge(b.t_q, qu + 1)
        if qu >= 2:
            e.wait_ge(b.q_qe[qp], 16 * (qu // 2))
        ps = b.PB if qp == 0 else b.PB2
        e.activation(b.qst[qp][:, :], ps[:, :], AF.Copy)
        e.drain()
        e.nop().then_inc(b.s_qe, 1)
        g512, hc = divmod(qu % 32, 8)
        e.dma_start(b.qfm[hc, :, utg * TG + g512 * 512:
                          utg * TG + (g512 + 1) * 512],
                    b.qst[qp][:, :]).then_inc(b.q_qe[qp], 16)
        qu += 1

    def r_unit(utg):
        nonlocal ru
        rp = ru % 2
        e.wait_ge(b.t_r, ru + 1)
        if ru == 0:
            e.wait_ge(b.q_g, 32)
        if ru >= 2:
            e.wait_ge(b.q_rd, 16 * ru)
        ps = b.PE if rp == 0 else b.PF
        e.activation(b.rst[rp][:, :], ps[0:E, :], AF.Identity,
                     bias=b.brsb[:, 0:1])
        e.drain()
        e.nop().then_inc(b.s_re, 1)
        g512 = ru % 4
        base = utg * TG + g512 * 512
        e.dma_start(b.logit[:, base:base + 512],
                    b.rst[rp][:, :]).then_inc(b.q_rd, 16)
        ru += 1

    def l_unit(g):
        e.wait_ge(b.t_lt, g + 1)
        pr = b.PB if g % 2 == 0 else b.PB2
        e.activation(b.ltm[:, g * 4 * E:(g + 1) * 4 * E], pr[:, 0:32],
                     AF.Copy).then_inc(b.s_lt, 1)

    queues = []
    for tg in range(16):
        q = []
        if _own(tg):
            for st in range(32):
                q.append(("K", tg))
                q.append(("V", tg))
                if tg < 2:
                    q.append(("Q", tg))
        q += [("R", tg)] * 4
        if _on("R") and 5 <= tg <= 15:
            q += [("L", 4 * (tg - 5) + gi) for gi in range(4)]
        queues.append(q)
    emit = {"K": k_unit, "V": v_unit, "Q": q_unit, "R": r_unit,
            "L": l_unit}

    for T in range(19):
        for i in range(16):
            if 1 <= T <= 16:
                m = 16 * (T - 1) + i
                par = m % 2
                if m % 2 == 0:
                    p = m // 2
                    c = 2 * (p % 8)
                    smp = b.statm[(m // 16) % 2]
                    e.wait_ge(b.v_bs, p + 1)
                    e.activation(smp[:, 48 + c:50 + c],
                                 smp[:, 32 + c:34 + c], AF.Sqrt)
                    e.drain()
                    e.nop().then_inc(b.s_sq, 1)
                e.wait_ge(b.v_xn, m + 1)
                if m >= 2:
                    e.wait_ge(b.q_st[par], 32 * (m // 2))
                e.activation(b.hitm[par][:, :], b.xr[m % 5][:, :],
                             AF.Copy).then_inc(b.s_hi, 1)
            if T >= 3:
                qq = queues[T - 3]
                for _ in range(min(7, len(qq))):
                    if qq[0][0] == "R" and i < 8:
                        break
                    kind, utg = qq.pop(0)
                    emit[kind](utg)
        if T >= 3:
            for kind, utg in queues[T - 3]:
                emit[kind](utg)
            queues[T - 3] = []

    if not _on("R"):
        return
    for g in range(44, 64):
        l_unit(g)
    e.drain()
    e.wait_ge(b.v_c, 1)
    e.activation(b.smtm[:, :], b.ltm[:, :], AF.Exp,
                 bias=b.cm12[:, 0:1]).then_inc(b.s_c, 1)
    for it in range(NITER):
        e.wait_ge(b.t_c2, 2 * it + 1)
        e.activation(b.cnt[0:1, :], b.PE[0:1, 0:E], AF.Copy).then_inc(b.s_c, 1)
        e.wait_ge(b.t_c2, 2 * it + 2)
        e.activation(b.threp[:, :], b.PF[:, 0:E], AF.Copy).then_inc(b.s_c, 1)
    e.wait_ge(b.t_c2, 2 * NITER + 1)
    e.activation(b.threp[:, :], b.PF[:, 0:E], AF.Copy).then_inc(b.s_c, 1)

    if not _on("G"):
        return
    for st in range(32):
        e.wait_ge(b.t_b, 2 * st + 1)
        if st >= 1:
            e.wait_ge(b.v_fin, st)
        e.activation(b.ustr[:, :], b.PF[:, 0:E], AF.Exp,
                     bias=b.cm12[:, 0:1]).then_inc(b.s_fin, 1)
        e.wait_ge(b.t_b, 2 * st + 2)
        e.activation(b.mfm[:, st * 128:(st + 1) * 128], b.PE[0:E, 0:128],
                     AF.Copy).then_inc(b.s_fin, 1)

    if not _on("B"):
        return
    for p in range(NPASS):
        for sb in range(NSB):
            for qg in range(NQG):
                for kt in range(8):
                    su = ((p * NSB + sb) * NQG + qg) * 8 + kt
                    sp = su % 2
                    e.wait_ge(b.t_s, su + 1)
                    ps = b.PC if sp == 0 else b.PD
                    e.activation(b.pT[sp][:, :], ps[:, 0:256], AF.Exp,
                                 bias=b.cm24[:, 0:1],
                                 scale=b.c0125[:, 0:1]).then_inc(b.s_e, 1)
        e.wait_ge(b.v_r, p + 1)
        for tile in range(16):
            gt = p * 16 + tile
            if gt >= 2:
                e.wait_ge(b.q_on[gt % 2], 16 * (gt // 2))
            e.activation(b.onst[gt % 2][:, :],
                         b.osb[:, tile * D:(tile + 1) * D], AF.Copy,
                         scale=b.lrec[:, tile:tile + 1])
            e.drain()
            e.nop().then_inc(b.s_n, 1)
            row = p * PQ + tile * 128
            e.dma_start(b.outn[row:row + 128, :],
                        b.onst[gt % 2][:, :]).then_inc(b.q_on[gt % 2], 16)

    if not _on("C1"):
        return
    for cu in range(32):
        e.wait_ge(b.v_yp, cu + 1)
        t, ddc = divmod(cu, 8)
        row = t * 1024 + ddc * 128
        e.dma_start(b.ypart[row:row + 128, :],
                    b.ypst[cu % 2][:, :]).then_inc(b.q_yp, 16)

    if not _on("C2"):
        return
    uu = 0
    e.wait_ge(b.q_c2c, 16 * 9)
    for gg in range(NG):
        for tt in range(8):
            for fi, (f0, fw) in enumerate(FCH):
                e.wait_ge(b.t_u, uu + 1)
                if fi == 0:
                    e.wait_ge(b.t_d, (gg * 8 + tt) * 8)
                ps = _c2_up_bank(b, gg, uu)
                e.activation(b.mid[0:fw, fi * 512:(fi + 1) * 512],
                             ps[0:fw, :], AF.Silu,
                             bias=b.bupsb[0:fw, gg * NFC + fi:gg * NFC + fi + 1]
                             ).then_inc(b.s_u, 1)
                uu += 1
            e.wait_ge(b.v_g, (gg * 8 + tt) * 8 + 8)
            t = tt // 2
            for tc in range(4):
                r = 4 * (tt % 2) + tc
                if gg < NG - 1:
                    row0 = t * 1024 + r
                    e.dma_start(b.ypart[row0:row0 + 1017:8, :],
                                b.ysb[:, tc * D:(tc + 1) * D]).then_inc(b.q_yp, 16)
                else:
                    row = tt * 512 + tc * 128
                    e.dma_start(b.y[row:row + 128, :],
                                b.ysb[:, tc * D:(tc + 1) * D]).then_inc(b.q_yp, 16)
    e.wait_ge(b.q_yp, 16 * (32 + NG * 32))


def _vector(b, e):
    e.wait_ge(b.g_c, 2)
    e.tensor_scalar(b.identf[:, :], b.colio[:, :], b.rowio[:, 0:1], None,
                    op0=ALU.is_equal)
    e.memset(b.ones1[:, :], 1.0)
    e.memset(b.onesc[:, :], 1.0)
    e.memset(b.onesb[:, :], 1.0)
    e.memset(b.theta[:, :], 0.5)
    e.memset(b.threp[:, :], 0.5)
    e.memset(b.cm12[:, :], -RSHIFT)
    e.memset(b.cm24[:, :], -SHIFT)
    e.memset(b.c0125[:, :], 0.125)
    e.wait_ge(b.q_g, 32)
    e.tensor_copy(b.wrhi[:, :], b.wrf[:, :])
    e.drain()
    e.tensor_tensor(b.wrlo[:, :], b.wrf[:, :], b.wrhi[:, :], op=ALU.subtract)
    e.engine_nop().then_inc(b.v_c, 1)

    xe = 0
    for k in range(259):
        tg, i = divmod(k, 16)
        m = k - 2
        if 0 <= m <= 255:
            mtg, mj = divmod(m, 16)
            smp = b.statm[mtg % 2]
            if m % 2 == 0:
                p = m // 2
                c = 2 * (p % 8)
                e.wait_ge(b.s_sq, p + 1)
                e.reciprocal(smp[:, 64 + c:66 + c], smp[:, 48 + c:50 + c])
                e.drain()
                e.scalar_tensor_tensor(smp[:, 80 + c:82 + c],
                                       smp[:, 2 * c:2 * c + 3:2], -1.0,
                                       smp[:, 64 + c:66 + c],
                                       op0=ALU.mult, op1=ALU.mult)
                e.drain()
            e.tensor_scalar(b.xr[m % 5][:, :], b.xr[m % 5][:, :],
                            smp[:, 64 + mj:65 + mj], smp[:, 80 + mj:81 + mj],
                            op0=ALU.mult, op1=ALU.add).then_inc(b.v_xn, 1)
            e.drain()
        sj = k - 3
        if 0 <= sj <= 255:
            e.wait_ge(b.s_hi, sj + 1)
            e.tensor_tensor(b.lotm[sj % 2][:, :], b.xr[sj % 5][:, :],
                            b.hitm[sj % 2][:, :],
                            op=ALU.subtract).then_inc(b.g_lo, 1)
        if k <= 255:
            par = k % 2
            sm = b.statm[tg % 2]
            e.wait_ge(b.q_xr[k % 5], 16 * (k // 5 + 1))
            e.bn_stats(b.bn12[par][:, 0:6], b.xr[k % 5][:, 0:512])
            e.bn_stats(b.bn12[par][:, 6:12], b.xr[k % 5][:, 512:1024])
            e.drain()
            e.bn_aggr(sm[:, 2 * i:2 * i + 2],
                      b.bn12[par][:, :]).then_inc(b.v_bn, 1)
            if k % 2 == 1:
                p = k // 2
                c = 2 * (p % 8)
                e.drain()
                e.tensor_scalar(sm[:, 32 + c:34 + c],
                                sm[:, 2 * c + 1:2 * c + 4:2], EPS, None,
                                op0=ALU.add)
                e.drain()
                e.engine_nop().then_inc(b.v_bs, 1)
        exn = (tg - 2) if i >= 12 else (tg - 3) if i < 4 else -1
        if 0 <= exn <= 7 and (i >= 12 or i < 4):
            ptg = exn
            if i == 12:
                e.wait_ge(b.q_tl[ptg % 2], 16 * 8 * (ptg // 2 + 1))
            for _ in range(4):
                u = xe
                t, r = divmod(u % 32, 8)
                xp = u % 2
                if u >= 2:
                    e.wait_ge(b.q_xs[u % 2], 16 * (u // 2))
                src = b.hifm[ptg % 2][:, :].rearrange("p (c t) -> p c t", t=TG)
                src = src[:, :, 16 * r + t::128]
                e.tensor_copy(
                    b.xist[xp][:, :].rearrange("p (c h) -> p c h", h=16),
                    src).then_inc(b.v_xe, 1)
                xe += 1

    if not _on("R"):
        return
    e.wait_ge(b.s_c, 1)
    sm3 = b.smtm[:, :].rearrange("p (c e) -> p c e", e=E)
    e.tensor_reduce(b.ztm[:, :], sm3, axis=AX.X, op=ALU.add)
    e.drain()
    e.reciprocal(b.ztm[:, :], b.ztm[:, :])
    e.drain()
    for ee in range(E):
        sl = b.smtm[:, :].rearrange("p (c e) -> p c e", e=E)[:, :, ee]
        e.tensor_tensor(sl, sl, b.ztm[:, :], op=ALU.mult)
    e.drain()
    e.engine_nop().then_inc(b.v_c, 1)
    delta = 0.25
    for it in range(NITER):
        if it >= 1:
            e.wait_ge(b.s_c, 1 + 2 * it)
        for ee in range(E):
            sl = b.smtm[:, :].rearrange("p (c e) -> p c e", e=E)[:, :, ee]
            if ee >= 1:
                e.drain()
            e.tensor_scalar(b.ind[:, :], sl, b.threp[:, ee:ee + 1], 0.0,
                            op0=ALU.is_ge, op1=ALU.add,
                            accum_out=b.part8[:, ee:ee + 1])
        e.drain()
        e.engine_nop().then_inc(b.v_c, 1)
        e.wait_ge(b.s_c, 2 + 2 * it)
        if it >= 1:
            e.wait_ge(b.t_c2, 2 * it)
        e.tensor_scalar(b.cnt[0:1, :], b.cnt[0:1, :], KSEL + 0.5, None,
                        op0=ALU.is_ge)
        e.drain()
        e.scalar_tensor_tensor(b.theta[0:1, :], b.cnt[0:1, :], 2 * delta,
                               b.theta[0:1, :], op0=ALU.mult, op1=ALU.add)
        e.drain()
        e.tensor_scalar(b.theta[0:1, :], b.theta[0:1, :], -delta, None,
                        op0=ALU.add).then_inc(b.v_c, 1)
        delta *= 0.5
    e.drain()
    e.wait_ge(b.t_c2, 2 * NITER)
    e.tensor_scalar(b.theta[0:1, :], b.theta[0:1, :], 6 * delta, None,
                    op0=ALU.add).then_inc(b.v_c, 1)

    if not _on("G"):
        return
    for st in range(32):
        e.wait_ge(b.s_fin, 2 * st + 1)
        e.tensor_reduce(b.z8[:, 0:1], b.ustr[:, :], axis=AX.X, op=ALU.add)
        e.drain()
        e.reciprocal(b.z8[:, 1:2], b.z8[:, 0:1])
        e.drain()
        sl = b.mtm[:, st * E:(st + 1) * E]
        e.tensor_scalar(sl, b.ustr[:, :], b.z8[:, 1:2], None, op0=ALU.mult)
        e.drain()
        e.tensor_tensor(b.g8[:, :], sl, b.threp[:, :], op=ALU.is_ge)
        e.drain()
        e.tensor_tensor(sl, sl, b.g8[:, :], op=ALU.mult).then_inc(b.v_fin, 1)

    if not _on("B"):
        return
    for p in range(NPASS):
        if p == 1:
            e.wait_ge(b.s_n, 16)
        else:
            for par in range(2):
                e.wait_ge(b.q_ke[par], 16 * 128)
                e.wait_ge(b.q_ve[par], 16 * 128)
                e.wait_ge(b.q_qe[par], 16 * 32)
                e.wait_ge(b.q_xs[par], 16 * 128)
        e.memset(b.osb[:, :], 0.0)
        e.memset(b.lsb[:, :], 0.0)
        e.engine_nop().then_inc(b.v_m, 1)
        for sb in range(NSB):
            for qg in range(NQG):
                pu = (p * NSB + sb) * NQG + qg
                e.wait_ge(b.t_pv, pu + 1)
                e.drain()
                t0 = 2 * qg
                e.tensor_tensor(b.osb[:, t0 * D:t0 * D + 512],
                                b.PA[:, :], b.osb[:, t0 * D:t0 * D + 512],
                                op=ALU.add)
                e.tensor_tensor(b.osb[:, t0 * D + 512:(t0 + 1) * D],
                                b.PA2[:, :],
                                b.osb[:, t0 * D + 512:(t0 + 1) * D],
                                op=ALU.add)
                e.tensor_tensor(b.osb[:, (t0 + 1) * D:(t0 + 1) * D + 512],
                                b.PB[:, :],
                                b.osb[:, (t0 + 1) * D:(t0 + 1) * D + 512],
                                op=ALU.add)
                e.tensor_tensor(b.osb[:, (t0 + 1) * D + 512:(t0 + 2) * D],
                                b.PB2[:, :],
                                b.osb[:, (t0 + 1) * D + 512:(t0 + 2) * D],
                                op=ALU.add)
                e.tensor_tensor(b.lsb[:, t0:t0 + 1], b.PE[:, qg:qg + 1],
                                b.lsb[:, t0:t0 + 1], op=ALU.add)
                e.tensor_tensor(b.lsb[:, t0 + 1:t0 + 2], b.PF[:, qg:qg + 1],
                                b.lsb[:, t0 + 1:t0 + 2],
                                op=ALU.add).then_inc(b.v_m, 1)
        e.drain()
        e.reciprocal(b.lrec[:, :], b.lsb[:, :])
        e.drain()
        e.engine_nop().then_inc(b.v_r, 1)

    if not _on("C1"):
        return
    for cu in range(32):
        e.wait_ge(b.t_c, cu + 1)
        e.wait_ge(b.q_g, _qg_xres(cu))
        cp = cu % 2
        pc = (b.PA, b.PA2) if cp == 0 else (b.PB, b.PB2)
        if cu >= 2:
            e.wait_ge(b.q_yp, 16 * cu)
        e.tensor_tensor(b.ypst[cp][:, 0:512], pc[0][:, :],
                        b.xres[cp][:, 0:512], op=ALU.add)
        e.tensor_tensor(b.ypst[cp][:, 512:1024], pc[1][:, :],
                        b.xres[cp][:, 512:1024], op=ALU.add).then_inc(b.v_yp, 1)

    if not _on("C2"):
        return
    for gg in range(NG):
        for tt in range(8):
            e.wait_ge(b.q_y, 16 * (gg * 32 + tt * 4 + 4))
            if gg == 0:
                for tc in range(4):
                    bb_ = tt * 4 + tc
                    e.wait_ge(b.t_b, 64 + bb_ + 1)
                    e.drain()
                    e.tensor_tensor(b.ysb[:, tc * D:tc * D + 512], b.PE[:, :],
                                    b.ysb[:, tc * D:tc * D + 512], op=ALU.add)
                    e.tensor_tensor(b.ysb[:, tc * D + 512:(tc + 1) * D],
                                    b.PF[:, :],
                                    b.ysb[:, tc * D + 512:(tc + 1) * D],
                                    op=ALU.add).then_inc(b.v_bb, 1)
            for ee in range(2):
                for tc in range(4):
                    du = ((gg * 8 + tt) * 2 + ee) * 4 + tc
                    dp = du % 2
                    pd = (b.PA, b.PA2) if dp == 0 else (b.PB, b.PB2)
                    t = tt // 2
                    r = 4 * (tt % 2) + tc
                    col = (t * 8 + r) * E + gg * 2 + ee
                    e.wait_ge(b.t_d, du + 1)
                    e.drain()
                    e.scalar_tensor_tensor(b.ysb[:, tc * D:tc * D + 512],
                                           pd[0][:, :], b.mtm[:, col:col + 1],
                                           b.ysb[:, tc * D:tc * D + 512],
                                           op0=ALU.mult, op1=ALU.add)
                    e.scalar_tensor_tensor(b.ysb[:, tc * D + 512:(tc + 1) * D],
                                           pd[1][:, :], b.mtm[:, col:col + 1],
                                           b.ysb[:, tc * D + 512:(tc + 1) * D],
                                           op0=ALU.mult,
                                           op1=ALU.add).then_inc(b.v_g, 1)


def _tensor(b, e):
    mm = e.matmul
    ku = vu = qu = ru = 0
    for tg in range(16):
        e.wait_ge(b.q_tl[tg % 2], 16 * 8 * (tg // 2 + 1))
        if tg == 0:
            e.wait_ge(b.q_wT, 16 * 24)
        hf = b.hifm[tg % 2]
        if _own(tg):
            for st in range(32):
                g512, hc = divmod(st, 8)
                tok = g512 * 512
                kp = ku % 2
                ps = b.PC if kp == 0 else b.PD
                if ku >= 2:
                    e.wait_ge(b.s_ke, ku - 1)
                for dc in range(8):
                    ins = mm(ps[:, :],
                             b.wkT[:, dc * D + hc * 128:dc * D + (hc + 1) * 128],
                             hf[:, dc * TG + tok:dc * TG + tok + 512],
                             start=(dc == 0), stop=(dc == 7))
                ins.then_inc(b.t_k, 1)
                ku += 1
                vp = vu % 2
                pv = b.PA if vp == 0 else b.PA2
                if vu >= 2:
                    e.wait_ge(b.s_ve, vu - 1)
                vtok = (st // 2) * 128
                h = st % 2
                for dc in range(8):
                    ins = mm(pv[:, :],
                             hf[:, dc * TG + vtok:dc * TG + vtok + 128],
                             b.wvT[:, dc * D + h * 512:dc * D + h * 512 + 512],
                             start=(dc == 0), stop=(dc == 7))
                ins.then_inc(b.t_v, 1)
                vu += 1
                if tg < 2:
                    qp = qu % 2
                    ps = b.PB if qp == 0 else b.PB2
                    if qu >= 2:
                        e.wait_ge(b.s_qe, qu - 1)
                    for dc in range(8):
                        ins = mm(ps[:, :],
                                 b.wqT[:, dc * D + hc * 128:dc * D + (hc + 1) * 128],
                                 hf[:, dc * TG + tok:dc * TG + tok + 512],
                                 start=(dc == 0), stop=(dc == 7))
                    ins.then_inc(b.t_q, 1)
                    qu += 1
        e.wait_ge(b.q_lo, 16 * 8 * (tg + 1))
        for g512 in range(4):
            tok = g512 * 512
            rp = ru % 2
            pr = b.PE if rp == 0 else b.PF
            if ru >= 2:
                e.wait_ge(b.s_re, ru - 1)
            for dc in range(8):
                mm(pr[0:E, :], b.wrhi[:, dc * E:(dc + 1) * E],
                   hf[:, dc * TG + tok:dc * TG + tok + 512],
                   start=(dc == 0), stop=False)
                mm(pr[0:E, :], b.wrlo[:, dc * E:(dc + 1) * E],
                   hf[:, dc * TG + tok:dc * TG + tok + 512],
                   start=False, stop=False)
                ins = mm(pr[0:E, :], b.wrhi[:, dc * E:(dc + 1) * E],
                         b.lofm[:, dc * TG + tok:dc * TG + tok + 512],
                         start=False, stop=(dc == 7))
            ins.then_inc(b.t_r, 1)
            ru += 1
        if _on("R") and 5 <= tg <= 15:
            if tg == 5:
                e.wait_ge(b.s_qe, 64)
            for gi in range(4):
                g = 4 * (tg - 5) + gi
                _stage_tr(b, e, g)

    if not _on("R"):
        return
    for g in range(44, 64):
        _stage_tr(b, e, g)
    for it in range(NITER):
        e.wait_ge(b.v_c, 3 + 2 * it)
        mm(b.PE[0:1, 0:E], b.onesc[:, :], b.part8[:, :], start=True,
           stop=True).then_inc(b.t_c2, 1)
        e.wait_ge(b.v_c, 4 + 2 * it)
        mm(b.PF[:, 0:E], b.ones1[0:1, :], b.theta[0:1, :], start=True,
           stop=True).then_inc(b.t_c2, 1)
    e.wait_ge(b.v_c, 3 + 2 * NITER)
    e.wait_ge(b.s_c, 2 * NITER + 1)
    mm(b.PF[:, 0:E], b.ones1[0:1, :], b.theta[0:1, :], start=True,
       stop=True).then_inc(b.t_c2, 1)

    if not _on("G"):
        return
    e.wait_ge(b.s_c, 2 * NITER + 2)
    for st in range(32):
        e.wait_ge(b.q_g, _qg_lstr(st))
        if st >= 1:
            e.wait_ge(b.s_fin, 2 * st)
        e.transpose(b.PF[:, 0:E], b.lstr[0:E, :],
                    b.identf[0:E, 0:E]).then_inc(b.t_b, 1)
        e.wait_ge(b.v_fin, st + 1)
        e.transpose(b.PE[0:E, 0:128], b.mtm[:, st * E:(st + 1) * E],
                    b.identf[:, :]).then_inc(b.t_b, 1)

    if not _on("B"):
        return
    for p in range(NPASS):
        for sb in range(NSB):
            g = p * NSB + sb
            par = g % 2
            if p == 0 and sb == 0 and _on("G"):
                e.wait_ge(b.s_fin, 64)
            e.wait_ge(b.q_qsb, 16 * 8 * (p + 1))
            e.wait_ge(b.q_kv[g % 2], 16 * 16 * (g // 2 + 1))
            for qg in range(NQG):
                pu = g * NQG + qg
                for kt in range(9):
                    if kt < 8:
                        su = pu * 8 + kt
                        sp = su % 2
                        if su >= 2:
                            e.wait_ge(b.s_e, su - 1)
                        ps = b.PC if sp == 0 else b.PD
                        for hc in range(8):
                            ins = mm(ps[:, 0:256],
                                     b.ksb[par][:, hc * SBK + kt * 128:
                                                hc * SBK + (kt + 1) * 128],
                                     b.qsb[:, hc * PQ + qg * 256:
                                           hc * PQ + (qg + 1) * 256],
                                     start=(hc == 0), stop=(hc == 7))
                        ins.then_inc(b.t_s, 1)
                    if kt >= 1:
                        ku2 = kt - 1
                        su = pu * 8 + ku2
                        sp = su % 2
                        e.wait_ge(b.s_e, su + 1)
                        if ku2 == 0:
                            e.wait_ge(b.v_m, p + 1 + pu)
                        stf = (ku2 == 0)
                        spf = (ku2 == 7)
                        for qc in range(2):
                            pt = b.pT[sp][:, qc * 128:(qc + 1) * 128]
                            po = (b.PA, b.PA2) if qc == 0 else (b.PB, b.PB2)
                            mm(po[0][:, :], pt,
                               b.vsb[par][:, ku2 * SBK:ku2 * SBK + 512],
                               start=stf, stop=spf)
                            mm(po[1][:, :], pt,
                               b.vsb[par][:, ku2 * SBK + 512:(ku2 + 1) * SBK],
                               start=stf, stop=spf)
                            pl = b.PE if qc == 0 else b.PF
                            ins = mm(pl[:, qg:qg + 1], pt,
                                     b.onesb[:, :], start=stf, stop=spf)
                ins.then_inc(b.t_pv, 1)

    if not _on("C1"):
        return
    e.wait_ge(b.q_wT, 16 * 32)
    for cu in range(32):
        t, ddc = divmod(cu, 8)
        cp = cu % 2
        pc = (b.PA, b.PA2) if cp == 0 else (b.PB, b.PB2)
        e.wait_ge(b.q_on2, 16 * 8 * (t + 1))
        if cu >= 2:
            e.wait_ge(b.v_yp, cu - 1)
        for jc in range(8):
            st = b.onld[:, jc * D + ddc * 128:jc * D + (ddc + 1) * 128]
            mm(pc[0][:, :], st, b.woT[:, jc * D:jc * D + 512],
               start=(jc == 0), stop=(jc == 7))
            ins = mm(pc[1][:, :], st, b.woT[:, jc * D + 512:(jc + 1) * D],
                     start=(jc == 0), stop=(jc == 7))
        ins.then_inc(b.t_c, 1)

    if not _on("C2"):
        return
    uu = 0
    du = 0
    e.wait_ge(b.q_c2c, 16 * 9)
    for gg in range(NG):
        for tt in range(8):
            xp = tt % 2
            gt = gg * 8 + tt
            e.wait_ge(b.q_xm[gt % 2], 16 * 32 * (gt // 2 + 1))
            if gg == 0:
                e.wait_ge(b.s_fin, 64)
                t = tt // 2
                for tc in range(4):
                    bb_ = tt * 4 + tc
                    if bb_ >= 1:
                        e.wait_ge(b.v_bb, bb_)
                    r = 4 * (tt % 2) + tc
                    st8 = (t * 8 + r) * 128
                    mm(b.PE[:, :], b.mfm[:, st8:st8 + 128], b.bdb[:, 0:512],
                       start=True, stop=True)
                    mm(b.PF[:, :], b.mfm[:, st8:st8 + 128], b.bdb[:, 512:1024],
                       start=True, stop=True).then_inc(b.t_b, 1)
            if gg == 1 and tt == 0:
                e.wait_ge(b.v_bb, 32)
            for fi, (f0, fw) in enumerate(FCH):
                if tt == 0 and fi == 0:
                    e.wait_ge(b.q_wq[0], 16 * (gg + 1))
                ps = _c2_up_bank(b, gg, uu)
                dep = 2 if gg == 0 else 4
                if uu >= dep:
                    e.wait_ge(b.s_u, uu - dep + 1)
                for dc in range(8):
                    ins = mm(ps[0:fw, :],
                             b.wupsb[0:128, dc * FF + f0:dc * FF + f0 + fw],
                             b.xmv[xp][:, dc * 512:(dc + 1) * 512],
                             start=(dc == 0), stop=(dc == 7))
                ins.then_inc(b.t_u, 1)
                uu += 1
            e.wait_ge(b.s_u, uu)
            for ee in range(2):
                if tt == 0:
                    e.wait_ge(b.q_wt if ee == 0 else b.q_wd,
                              16 * 2 * (gg + 1))
                for tc in range(4):
                    dp = du % 2
                    pd = (b.PA, b.PA2) if dp == 0 else (b.PB, b.PB2)
                    if du >= 2:
                        e.wait_ge(b.v_g, du - 1)
                    for fi, (f0, fw) in enumerate(FCH):
                        st = b.mid[0:fw, fi * 512 + tc * 128:
                                   fi * 512 + (tc + 1) * 128]
                        mm(pd[0][:, :], st,
                           b.wdsb[ee][0:fw, fi * D:fi * D + 512],
                           start=(fi == 0), stop=(fi == NFC - 1))
                        ins = mm(pd[1][:, :], st,
                                 b.wdsb[ee][0:fw, fi * D + 512:(fi + 1) * D],
                                 start=(fi == 0), stop=(fi == NFC - 1))
                    ins.then_inc(b.t_d, 1)
                    du += 1


def _gpsimd(b, e):
    e.iota(b.colio[:, :], pattern=[[1, 128]], base=0, channel_multiplier=0,
           allow_small_or_imprecise_dtypes=True).then_inc(b.g_c, 1)
    e.iota(b.rowio[:, :], pattern=[[1, 1]], base=0, channel_multiplier=1,
           allow_small_or_imprecise_dtypes=True).then_inc(b.g_c, 1)
    e.dma_start(b.wrf[:, :].rearrange("p (c e) -> p c e", e=E),
                b.wr[:, :].rearrange("(c p) e -> c p e", p=128)
                .rearrange("c p e -> p c e")).then_inc(b.q_g, 16)
    e.dma_start(b.brsb[:, :], b.br[:, :]).then_inc(b.q_g, 16)
    for j, w in enumerate([b.wq, b.wk, b.wv, b.wo]):
        e.dma_start(b.wld[:, :].rearrange("p (c d) -> p c d", d=D),
                    w[:, :].rearrange("(c p) d -> c p d", p=128)
                    .rearrange("c p d -> p c d")).then_inc(b.q_wl, 16)
        e.wait_ge(b.q_wl, 16 * (j + 1))
        e.dma_start(b.wnat[j, :, :].rearrange("(c p) d -> c p d", p=128)
                    .rearrange("c p d -> p c d"),
                    b.wld[:, :].rearrange("p (c d) -> p c d", d=D)
                    ).then_inc(b.q_w, 16)
        e.wait_ge(b.q_w, 16 * (j + 1))
    xn_at = {}
    for u in range(256):
        ptg, j = divmod(u, 32)
        tile = 16 * (ptg + 2) + 12 + j // 4 if j < 16 else \
            16 * (ptg + 3) + j // 4 - 4
        xn_at.setdefault(tile + 1, []).append(u)
    for k in range(260):
        if k <= 255:
            if k >= 5:
                e.wait_ge(b.g_lo, k - 4)
            e.dma_start(b.xr[k % 5][:, :], _xsrc(b, k)
                        ).then_inc(b.q_xr[k % 5], 16)
        for u in xn_at.get(k, ()):
            e.wait_ge(b.v_xe, u + 1)
            tg8 = u // 32
            t, r = divmod(u % 32, 8)
            e.dma_start(b.xnic[t, r, tg8, :, :], b.xist[u % 2][:, :]
                        ).then_inc(b.q_xs[u % 2], 16)
        j = k - 4
        if 0 <= j <= 255:
            if j % 16 == 0 and j >= 16:
                e.wait_ge(b.s_ts, j // 16)
            e.wait_ge(b.g_lo, j + 1)
            pp = j % 2
            dst = j * 128
            e.dma_start(b.xnhi[dst:dst + 128, :],
                        b.hitm[pp][:, :]).then_inc(b.q_st[pp], 16)
            e.dma_start(b.xnlo[dst:dst + 128, :],
                        b.lotm[pp][:, :]).then_inc(b.q_st[pp], 16)

    if not _on("G"):
        return
    e.wait_ge(b.s_c, 2 * NITER + 2)
    e.wait_ge(b.q_rd, 16 * 64)
    for st in range(32):
        t, r = divmod(st, 8)
        off = 16 * r + t
        if st >= 1:
            e.wait_ge(b.t_b, 2 * st - 1)
        e.dma_start(b.lstr[:, :], b.logit[:, off:off + S:128],
                    allow_slow_non_contiguous=True).then_inc(b.q_g, 16)

    if not _on("C1"):
        return
    e.wait_ge(b.t_r, 64)
    e.wait_ge(b.q_on[0], 16 * 16)
    e.wait_ge(b.q_on[1], 16 * 16)
    for cu in range(32):
        t, ddc = divmod(cu, 8)
        row0 = 16 * (ddc * 128) + t
        if cu >= 1:
            e.wait_ge(b.v_yp, cu)
        e.dma_start(b.xres[cu % 2][:, :],
                    b.x_own[row0:row0 + 2033:16, :]).then_inc(b.q_g, 16)

    if not _on("C2"):
        return
    e.wait_ge(b.t_c, 32)
    e.wait_ge(b.v_yp, 32)
    e.wait_ge(b.q_yp, 16 * 32)
    e.wait_ge(b.s_fin, 64)
    e.dma_start(b.bdb[:, :], b.bdn[:, :]).then_inc(b.q_c2c, 16)
    for g in range(NG):
        e.dma_start(b.bupsb[:, g * NFC:g * NFC + 21],
                    b.bup[g, 0:2688].rearrange("(c p) -> c p", p=128)
                    .rearrange("c p -> p c"),
                    allow_slow_non_contiguous=True).then_inc(b.q_c2c, 16)
        e.dma_start(b.bupsb[0:42, g * NFC + 21:g * NFC + 22],
                    b.bup[g, 2688:2730].rearrange("(f o) -> f o", o=1)
                    ).then_inc(b.q_c2c, 16)
    for gg in range(NG):
        if gg >= 1:
            e.wait_ge(b.t_u, gg * 8 * NFC)
            e.wait_ge(b.t_d, gg * 64)
        e.dma_start(b.wupsb[:, :].rearrange("p (c f) -> p c f", f=FF),
                    b.wup[gg, :, :].rearrange("(c p) f -> c p f", p=128)
                    .rearrange("c p f -> p c f")).then_inc(b.q_wq[0], 16)
        for ee in range(2):
            ex = 2 * gg + ee
            sw = b.q_wt if ee == 0 else b.q_wd
            e.dma_start(b.wdsb[ee][:, 0:21 * D].rearrange("p (c d) -> p c d", d=D),
                        b.wdn[ex, 0:2688, :].rearrange("(c p) d -> c p d", p=128)
                        .rearrange("c p d -> p c d")).then_inc(sw, 16)
            e.dma_start(b.wdsb[ee][0:42, 21 * D:22 * D],
                        b.wdn[ex, 2688:2730, :]).then_inc(sw, 16)
        for tt in range(8):
            xp = tt % 2
            if gg * 8 + tt >= 2:
                e.wait_ge(b.t_u, (gg * 8 + tt - 1) * NFC)
            t = tt // 2
            for dc in range(8):
                for rr in range(4):
                    r = 4 * (tt % 2) + rr
                    e.dma_start(
                        b.xmv[xp][:, dc * 512 + rr * 128:dc * 512 + (rr + 1) * 128]
                        .rearrange("p (g h) -> p g h", h=16),
                        b.xnic[t, r, :, :, dc * 16:(dc + 1) * 16]
                        .rearrange("g p h -> p g h")
                        ).then_inc(b.q_xm[(gg * 8 + tt) % 2], 16)


def kernel(L0_attn_q, L0_attn_wq, L0_attn_wk, L0_attn_wv, L0_attn_wo,
           L0_ffn_Wr, L0_ffn_br, L0_ffn_W_up, L0_ffn_b_up, L0_ffn_W_down,
           L0_ffn_b_down):
    x = np.asarray(L0_attn_q)
    nc = build()
    in_maps = []
    for c in range(8):
        bb, a = c // 4, c % 4
        xb = np.ascontiguousarray(
            np.roll(np.roll(x[bb].reshape(1024, 16, D), -256 * a, axis=0),
                    -4 * a, axis=1).reshape(S, D))
        wv_p = np.ascontiguousarray(np.roll(np.asarray(L0_attn_wv), -256 * a,
                                            axis=0))
        wo_p = np.ascontiguousarray(
            np.roll(np.asarray(L0_attn_wo).reshape(D, 64, 16), -4 * a,
                    axis=2).reshape(D, D))
        in_maps.append({
            "x_own": xb, "x_oth": np.ascontiguousarray(x[1 - bb]),
            "wq": np.asarray(L0_attn_wq), "wk": np.asarray(L0_attn_wk),
            "wv": wv_p, "wo": wo_p,
            "wr": np.asarray(L0_ffn_Wr),
            "br": np.asarray(L0_ffn_br).reshape(E, 1),
            "wup": np.asarray(L0_ffn_W_up), "bup": np.asarray(L0_ffn_b_up),
            "wdn": np.asarray(L0_ffn_W_down), "bdn": np.asarray(L0_ffn_b_down),
        })
    res = run_bass_kernel_spmd(nc, in_maps, core_ids=list(range(8)))
    out = np.empty((2, S, D), dtype=np.float32)
    for c in range(8):
        bb, a = c // 4, c % 4
        yc = np.asarray(res.results[c]["y"])
        tmp = yc.reshape(4, 8, 128, D).transpose(0, 2, 1, 3).reshape(4, 1024, D)
        part = np.moveaxis(tmp, 0, 1)
        rolled = np.roll(part, 256 * a, axis=0)
        out[bb].reshape(1024, 16, D)[:, 4 * a:4 * a + 4, :] = rolled
    return out.reshape(2, S, D)
